# version 14
# baseline (speedup 1.0000x reference)
import math
import numpy as np
import ml_dtypes
import concourse.bass as bass
import concourse.mybir as mybir
from concourse.bass_utils import run_bass_kernel_spmd

F32 = mybir.dt.float32
BF16 = mybir.dt.bfloat16
AF = mybir.ActivationFunctionType
ALU = mybir.AluOpType
AX = mybir.AxisListType

N_CORES = 8
D = 4096
R_CORE = 1024
R_HALF = 512
I_FFN = 11008
NIC = I_FFN // 128
NDC = D // 128
L = 4096
EPS = 1e-5


def _hilo(x64):
    hi = x64.astype(np.float32)
    lo = (x64 - hi.astype(np.float64)).astype(np.float32)
    return hi, lo


def make_constants():
    c = {}
    j = np.arange(L, dtype=np.float64)
    tanv = np.tan(np.pi * j / L)
    c["tan_hi"], c["tan_lo"] = _hilo(tanv.reshape(1, L))
    c["tan2d_hi"], c["tan2d_lo"] = _hilo(tanv.reshape(64, 64))
    a = np.arange(64, dtype=np.float64)
    ang64 = 2.0 * np.pi * np.outer(a, a) / 64.0
    c["fc_hi"], c["fc_lo"] = _hilo(np.cos(ang64))
    c["fs_hi"], c["fs_lo"] = _hilo(np.sin(ang64))
    angT = 2.0 * np.pi * np.outer(a, a) / 4096.0
    c["tc_hi"], c["tc_lo"] = _hilo(np.cos(angT) / 4096.0)
    c["ts_hi"], c["ts_lo"] = _hilo(np.sin(angT) / 4096.0)
    c["eye64"] = np.eye(64, dtype=np.float32)
    c["eye128"] = np.eye(128, dtype=np.float32)
    c["ones_k"] = np.ones((128, 1), dtype=np.float32)
    c["ones_r"] = np.ones((1, 128), dtype=np.float32)
    return c


ENGINES = ("sync", "gpsimd", "scalar", "tensor", "vector")


class Em:

    def __init__(self, nc):
        self.nc = nc
        self.sems = {}
        self.counts = {}
        self.events = {e: [] for e in ENGINES}
        self.last_wait = {e: {} for e in ENGINES}
        self.same_engine_sync = True

    def sem(self, name):
        if name not in self.sems:
            self.sems[name] = self.nc.alloc_semaphore(name)
            self.counts[name] = 0
        return self.sems[name]

    def on(self, engine, fn, deps=(), inc=None, inc_amt=1, tag=""):
        waits = {}
        for t in deps:
            if t is None:
                continue
            s, v, src = t
            if src == engine and not self.same_engine_sync:
                continue
            if self.last_wait[engine].get(s, -1) >= v:
                continue
            if waits.get(s, -1) < v:
                waits[s] = v
        for s, v in waits.items():
            self.last_wait[engine][s] = v
        tok = None
        if inc is not None:
            self.sem(inc)
            self.counts[inc] += inc_amt
            tok = (inc, self.counts[inc], engine)
        self.events[engine].append((fn, sorted(waits.items()), inc, inc_amt, tag))
        return tok

    def dma(self, queue, out, in_, deps=(), inc=None, tag=""):
        def fn(eng, out=out, in_=in_):
            return eng.dma_start(out=out, in_=in_)
        return self.on(queue, fn, deps=deps, inc=inc, inc_amt=16, tag=tag)

    def emit(self, engine_obj, engine_name):
        for fn, waits, inc, inc_amt, tag in self.events[engine_name]:
            for s, v in waits:
                engine_obj.wait_ge(self.sems[s], v)
            ins = fn(engine_obj)
            if inc is not None:
                if ins is None:
                    ins = engine_obj.nop()
                ins.then_inc(self.sems[inc], inc_amt)

    def finalize(self, block):
        for name in ENGINES:
            if not self.events[name]:
                continue
            getattr(block, name)(
                lambda eng, _n=name: self.emit(eng, _n)
            )


class Builder:
    def __init__(self, debug_ph0=False, lite=False, debug_int=False):
        nc = bass.Bass("TRN2", target_bir_lowering=False, dynamic_dma_scratch_size=512)
        self.nc = nc
        self.debug_ph0 = debug_ph0
        self.debug_int = debug_int
        self.lite = lite
        self.em = Em(nc)
        self.t = {}

        td = self.t
        def din(name, shape):
            td[name] = nc.dram_tensor(name, list(shape), F32, kind="ExternalInput").ap()
        if not lite:
            din("ut", (D, R_CORE))
            din("wgT", (D, 2 * I_FFN))
            din("wdT", (I_FFN, D))
        din("lam", (64, 2))
        din("pri", (64, 2))
        din("qri", (64, 2))
        din("vc", (64, 64, 2))
        din("ct", (1, 64, 2))
        din("bp", (64, 1))
        din("dskip", (1,))
        din("lstep", (1,))
        for k, v in make_constants().items():
            din(k, v.shape)

        td["out"] = nc.dram_tensor("out", [R_CORE, D], F32, kind="ExternalOutput").ap()
        td["g_dram"] = nc.dram_tensor("g_dram", [2 * L], F32, kind="Internal").ap()
        td["k_dram"] = nc.dram_tensor("k_dram", [8, L], F32, kind="Internal").ap()
        td["t_dram"] = nc.dram_tensor("t_dram", [R_HALF, D], F32, kind="Internal").ap()
        if debug_ph0:
            td["k_out"] = nc.dram_tensor("k_out", [L], F32, kind="ExternalOutput").ap()
            td["sh_out"] = nc.dram_tensor("sh_out", [128, L], F32, kind="ExternalOutput").ap()

        def sb(name, shape, dtype):
            td[name] = nc.alloc_sbuf_tensor(name, list(shape), dtype).ap()
            return td[name]

        sb("x1", [128, NDC * R_CORE], BF16)
        sb("hpool", [128, NIC * R_HALF], BF16)
        sb("sh", [128, L], BF16)
        sb("wstage", [128, 8 * 512], F32)
        sb("wbuf", [128, 4 * 512], BF16)
        sb("sg", [128, 2 * 512], BF16)
        sb("scratch", [128, 4096], F32)
        sb("scratch2", [128, 2048], F32)
        sb("small", [128, 1024], F32)

        self.psum = [nc.alloc_psum_tensor(f"pb{i}", [128, 512], F32).ap()
                     for i in range(8)]

    def sb(self, name, shape, dtype=F32):
        if name not in self.t:
            self.t[name] = self.nc.alloc_sbuf_tensor(name, list(shape), dtype).ap()
        return self.t[name]

    def build_ph0(self):
        nc, em, td = self.nc, self.em, self.t
        sb = self.sb

        v = lambda fn, deps, inc="ph0_v": em.on("vector", fn, deps=deps, inc=inc)
        s_ = lambda fn, deps, inc="ph0_s": em.on("scalar", fn, deps=deps, inc=inc)
        pe = lambda fn, deps, inc="ph0_pe": em.on("tensor", fn, deps=deps, inc=inc)
        self._plid = 0
        def ld(dst, src, deps=(), tag=""):
            self._plid += 1
            return em.dma("sync", dst, src, deps=deps,
                          inc=f"pl{self._plid}", tag=tag)

        lamt = sb("lamt", [64, 2]); prit = sb("prit", [64, 2]); qrit = sb("qrit", [64, 2])
        ctt = sb("ctt", [64, 2]); bpt = sb("bpt", [64, 1]); lstept = sb("lstept", [1, 1])
        dskipt = sb("dskipt", [1, 1])
        vct = sb("vct", [64, 128])
        eye64t = sb("eye64t", [64, 64])
        onesk = sb("onesk", [128, 1]); onesr = sb("onesr", [1, 128])
        fc = sb("fc", [64, 64]); fs = sb("fs", [64, 64]); fsn = sb("fsn", [64, 64])
        tc = sb("tc", [64, 64]); tsw = sb("tsw", [64, 64])
        tan2d = sb("tan2d", [64, 64])
        tmp64 = sb("tmp64", [64, 64])
        tmp64b = sb("tmp64b", [64, 64])
        tansum = td["scratch"][0:1, 0:L]
        tanlo_row = td["scratch"][0:1, 0:L]

        t0 = ld(lamt[:, :], td["lam"][:, :], tag="lam")
        t1 = ld(prit[:, :], td["pri"][:, :], tag="pri")
        t2 = ld(qrit[:, :], td["qri"][:, :], tag="qri")
        t3 = ld(ctt[:, :], td["ct"][0, :, :], tag="ct")
        t4 = ld(bpt[:, :], td["bp"][:, :], tag="bp")
        t5 = ld(lstept[:, :], td["lstep"].rearrange("(o a) -> o a", o=1), tag="lstep")
        self.tok_dskip = ld(dskipt[:, :], td["dskip"].rearrange("(o a) -> o a", o=1), tag="dskip")
        t6 = ld(vct[:, :], td["vc"].rearrange("a b c -> a (b c)"), tag="vc")
        t7 = ld(eye64t[:, :], td["eye64"][:, :], tag="eye64")
        self.tok_onesk = ld(onesk[:, :], td["ones_k"][:, :], tag="onesk")
        t9 = ld(onesr[:, :], td["ones_r"][:, :], tag="onesr")
        tanlo_tmp = td["scratch2"][0:1, 0:2048]
        t10 = ld(tansum, td["tan_hi"][:, :], tag="tanhi")
        t11a = ld(tanlo_tmp, td["tan_lo"][:, 0:2048], tag="tanlo_a")
        t12 = ld(fc[:, :], td["fc_hi"][:, :], tag="fchi")
        t13 = ld(tmp64[:, :], td["fc_lo"][:, :], tag="fclo")
        t20 = ld(tan2d[:, :], td["tan2d_hi"][:, :], tag="t2dhi")
        t21 = ld(tmp64b[:, :], td["tan2d_lo"][:, :], tag="t2dlo")

        c_tan_a = v(lambda e: e.tensor_add(tansum[:, 0:2048], tansum[:, 0:2048],
                                           tanlo_tmp), [t10, t11a])
        t11b = ld(tanlo_tmp, td["tan_lo"][:, 2048:4096], deps=[c_tan_a], tag="tanlo_b")
        c_tan = v(lambda e: e.tensor_add(tansum[:, 2048:4096], tansum[:, 2048:4096],
                                         tanlo_tmp), [t11b])
        c_fc = v(lambda e: e.tensor_add(fc[:, :], fc[:, :], tmp64[:, :]), [t12, t13])
        c_t2d = v(lambda e: e.tensor_add(tan2d[:, :], tan2d[:, :], tmp64b[:, :]), [t20, t21])
        t14 = ld(fs[:, :], td["fs_hi"][:, :], tag="fshi")
        t15 = ld(tmp64[:, :], td["fs_lo"][:, :], deps=[c_fc], tag="fslo")
        c_fs = v(lambda e: e.tensor_add(fs[:, :], fs[:, :], tmp64[:, :]), [t14, t15])
        c_fsn = v(lambda e: e.tensor_scalar_mul(fsn[:, :], fs[:, :], -1.0), [c_fs])
        t16 = ld(tc[:, :], td["tc_hi"][:, :], tag="tchi")
        t17 = ld(tmp64[:, :], td["tc_lo"][:, :], deps=[c_fs], tag="tclo")
        c_tc = v(lambda e: e.tensor_add(tc[:, :], tc[:, :], tmp64[:, :]), [t16, t17])
        t18 = ld(tsw[:, :], td["ts_hi"][:, :], tag="tshi")
        t19 = ld(tmp64[:, :], td["ts_lo"][:, :], deps=[c_tc], tag="tslo")
        c_ts = v(lambda e: e.tensor_add(tsw[:, :], tsw[:, :], tmp64[:, :]), [t18, t19])

        tscale = sb("tscale", [1, 1])
        c_sc = s_(lambda e: e.activation(tscale[:, :], lstept[:, :], AF.Exp,
                                         scale=-1.0), [t5])
        tsc64 = sb("tsc64", [64, 1])
        ps_sc = self.psum[6][0:64, 0:1]
        m_sc = pe(lambda e: e.matmul(ps_sc, onesr[0:1, 0:64], tscale[:, :],
                                     start=True, stop=True), [c_sc, t9])
        c_tau = v(lambda e: e.tensor_scalar_mul(tsc64[:, :], ps_sc, 2.0), [m_sc])

        lre = sb("lre", [64, 1]); lim = sb("lim", [64, 1])
        neg_lre = sb("neg_lre", [64, 1]); lre2 = sb("lre2", [64, 1])
        c_l0 = v(lambda e: e.tensor_copy(lre[:, :], lamt[:, 0:1]), [t0])
        c_l1 = v(lambda e: e.tensor_copy(lim[:, :], lamt[:, 1:2]), [t0])
        c_l2 = v(lambda e: e.tensor_scalar_mul(neg_lre[:, :], lre[:, :], -1.0), [c_l0])
        c_l3 = v(lambda e: e.tensor_mul(lre2[:, :], lre[:, :], lre[:, :]), [c_l0])

        vcre = sb("vcre", [64, 64]); vcim = sb("vcim", [64, 64])
        c_v0 = v(lambda e: e.tensor_copy(vcre[:, :], vct[:, 0:128:2]), [t6])
        c_v1 = v(lambda e: e.tensor_copy(vcim[:, :], vct[:, 1:128:2]), [t6])
        ps_t = self.psum[4][0:64, 0:64]
        vctre = sb("vctre", [64, 64]); vctim = sb("vctim", [64, 64])
        m_t0 = pe(lambda e: e.transpose(ps_t, vcre[:, :], eye64t[:, :]), [c_v0, t7],
                  inc="ph0_pe")
        c_v2 = v(lambda e: e.tensor_copy(vctre[:, :], ps_t), [m_t0])
        m_t1 = pe(lambda e: e.transpose(ps_t, vcim[:, :], eye64t[:, :]), [c_v1, c_v2])
        c_v3 = v(lambda e: e.tensor_copy(vctim[:, :], ps_t), [m_t1])
        ps_b = self.psum[5][0:64, 0:2]
        m_b0 = pe(lambda e: e.matmul(ps_b[:, 0:1], vctre[:, :], bpt[:, :],
                                     start=True, stop=True), [c_v2, t4])
        m_b1 = pe(lambda e: e.matmul(ps_b[:, 1:2], vctim[:, :], bpt[:, :],
                                     start=True, stop=True), [c_v3])
        bre = sb("bre", [64, 1]); bim = sb("bim", [64, 1])
        c_b0 = v(lambda e: e.tensor_copy(bre[:, :], ps_b[:, 0:1]), [m_b0])
        c_b1 = v(lambda e: e.tensor_copy(bim[:, :], ps_b[:, 1:2]), [m_b1])

        v8a = sb("v8a", [64, 8]); v8b = sb("v8b", [64, 8])
        ctre, ctim = ctt[:, 0:1], ctt[:, 1:2]
        pre_, pim_ = prit[:, 0:1], prit[:, 1:2]
        qre_, qim_ = qrit[:, 0:1], qrit[:, 1:2]
        tmpa = sb("tmpa", [64, 1]); tmpb = sb("tmpb", [64, 1])
        vdeps = [t1, t2, t3, c_b0, c_b1]

        def cprod(col, are, aim, brr, bii, dep_extra):
            d = vdeps + dep_extra
            tk1 = v(lambda e: e.tensor_mul(tmpa[:, :], are, brr), d)
            tk2 = v(lambda e: e.tensor_mul(tmpb[:, :], aim, bii), [tk1])
            tk3 = v(lambda e: e.tensor_add(v8a[:, col:col + 1], tmpa[:, :], tmpb[:, :]),
                    [tk2])
            tk4 = v(lambda e: e.tensor_mul(tmpa[:, :], are, bii), [tk3])
            tk5 = v(lambda e: e.tensor_mul(tmpb[:, :], aim, brr), [tk4])
            tk6 = v(lambda e: e.tensor_sub(v8a[:, col + 4:col + 5], tmpa[:, :], tmpb[:, :]),
                    [tk5])
            return tk6
        k0 = cprod(0, ctre, ctim, bre[:, :], bim[:, :], [])
        k1 = cprod(1, ctre, ctim, pre_, pim_, [k0])
        k2 = cprod(2, qre_, qim_, bre[:, :], bim[:, :], [k1])
        k3 = cprod(3, qre_, qim_, pre_, pim_, [k2])
        k4 = v(lambda e: e.tensor_copy(v8b[:, 0:4], v8a[:, 4:8]), [k3])
        k5 = v(lambda e: e.tensor_scalar_mul(v8b[:, 4:8], v8a[:, 0:4], -1.0), [k3])

        ps_tb = [self.psum[0][0:64, :], self.psum[1][0:64, :]]
        ps_k = [self.psum[2][0:8, :], self.psum[3][0:8, :]]
        s2 = td["scratch2"]
        r_dim = s2[0:64, 0:512]; r_recip = s2[0:64, 512:1024]
        r_m2 = s2[0:64, 1024:1536]; r_rimn = s2[0:64, 1536:2048]
        dim_sb = r_dim; m2a = r_m2; recip = r_recip
        rre = r_m2; rimn = r_rimn
        ktmp0 = td["small"][0:8, 0:512]
        ktmp = [ktmp0, ktmp0]
        kd = td["k_dram"]
        NCH = L // 512
        prev_done = [None, None]
        prev_dve = [k4, k5, c_tau, c_l1, c_l2, c_l3]
        prev_pe_read = [None, None]
        for ch in range(NCH):
            sl = ch % 2
            pb = ps_tb[sl]
            pk = ps_k[sl]
            jlo = ch * 512
            mtau = pe(lambda e, pb=pb, jlo=jlo: e.matmul(
                pb, onesr[0:1, 0:64], tansum[:, jlo:jlo + 512],
                start=True, stop=True),
                [c_tan, t9] + ([prev_pe_read[sl]] if prev_pe_read[sl] else []))
            d1 = v(lambda e, pb=pb: e.tensor_scalar(
                dim_sb, pb, tsc64[0:64, 0:1], lim[0:64, 0:1],
                ALU.mult, ALU.subtract),
                [mtau, c_tau] + prev_dve)
            d2 = v(lambda e: e.tensor_mul(m2a, dim_sb, dim_sb), [d1])
            d3 = v(lambda e: e.tensor_scalar(r_rimn, m2a, lre2[0:64, 0:1], None,
                                             ALU.add), [d2])
            d4 = v(lambda e: e.reciprocal(recip, r_rimn), [d3])
            d5 = v(lambda e: e.tensor_scalar(rre, recip, neg_lre[0:64, 0:1], None,
                                             ALU.mult), [d4])
            d6 = v(lambda e: e.tensor_mul(rimn, dim_sb, recip), [d5])
            prev_pe_read[sl] = d1
            mk1 = pe(lambda e, pk=pk: e.matmul(pk, v8a[0:64, :], rre, start=True,
                                               stop=False), [d5, d6])
            mk2 = pe(lambda e, pk=pk: e.matmul(pk, v8b[0:64, :], rimn, start=False,
                                               stop=True), [])
            ev = s_(lambda e, pk=pk, sl=sl: e.copy(ktmp[sl][:, :], pk),
                    [mk2] + ([prev_done[sl]] if prev_done[sl] else []))
            st = em.dma("sync", kd[0:8, jlo:jlo + 512], ktmp[sl][:, :], deps=[ev],
                        inc="kst", tag=f"kst{ch}")
            prev_done[0] = prev_done[1] = st
            prev_dve = [d6]
        kst_toks = list(prev_done)

        names = ["k00re", "k01re", "k10re", "k11re", "k00im", "k01im", "k10im", "k11im"]
        kt = {}
        ktoks = []
        for i, nm in enumerate(names):
            tile = sb("kt_" + nm, [64, 64])
            tok = ld(tile[:, :], kd[i, :].rearrange("(a b) -> a b", a=64),
                     deps=kst_toks, tag="kld" + nm)
            kt[nm] = tile
            ktoks.append(tok)

        if self.debug_ph0:
            w1 = sb("w1", [64, 64]); w2 = sb("w2", [64, 64])
            w3 = sb("w3", [64, 64]); w4 = sb("w4", [64, 64])
        else:
            w1 = vcre; w2 = vcim; w3 = vctre; w4 = vctim
        w5 = sb("w5", [64, 64])
        arr_re = sb("arr_re", [64, 64]); arr_im = sb("arr_im", [64, 64])
        g = lambda nm: kt[nm][:, :]
        deps0 = ktoks + [c_t2d]
        x1_ = v(lambda e: e.tensor_scalar_add(w1[:, :], g("k11re"), 1.0), deps0)
        x2_ = v(lambda e: e.tensor_mul(w2[:, :], w1[:, :], w1[:, :]), [x1_])
        x3_ = v(lambda e: e.tensor_mul(w3[:, :], g("k11im"), g("k11im")), [x2_])
        x4_ = v(lambda e: e.tensor_add(w2[:, :], w2[:, :], w3[:, :]), [x3_])
        x5_ = v(lambda e: e.reciprocal(w2[:, :], w2[:, :]), [x4_])
        x6_ = v(lambda e: e.tensor_mul(w3[:, :], g("k01re"), g("k10re")), [x5_])
        x7_ = v(lambda e: e.tensor_mul(w4[:, :], g("k01im"), g("k10im")), [x6_])
        x8_ = v(lambda e: e.tensor_sub(w3[:, :], w3[:, :], w4[:, :]), [x7_])
        x9_ = v(lambda e: e.tensor_mul(w4[:, :], g("k01re"), g("k10im")), [x8_])
        xa_ = v(lambda e: e.tensor_mul(w5[:, :], g("k01im"), g("k10re")), [x9_])
        xb_ = v(lambda e: e.tensor_add(w4[:, :], w4[:, :], w5[:, :]), [xa_])
        xc_ = v(lambda e: e.tensor_mul(w5[:, :], w3[:, :], w1[:, :]), [xb_])
        xd_ = v(lambda e: e.tensor_mul(arr_re[:, :], w4[:, :], g("k11im")), [xc_])
        xe_ = v(lambda e: e.tensor_add(w5[:, :], w5[:, :], arr_re[:, :]), [xd_])
        xf_ = v(lambda e: e.tensor_mul(w5[:, :], w5[:, :], w2[:, :]), [xe_])
        xg_ = v(lambda e: e.tensor_mul(w4[:, :], w4[:, :], w1[:, :]), [xf_])
        xh_ = v(lambda e: e.tensor_mul(w3[:, :], w3[:, :], g("k11im")), [xg_])
        xi_ = v(lambda e: e.tensor_sub(w4[:, :], w4[:, :], w3[:, :]), [xh_])
        xj_ = v(lambda e: e.tensor_mul(w4[:, :], w4[:, :], w2[:, :]), [xi_])
        xk_ = v(lambda e: e.tensor_sub(w1[:, :], g("k00re"), w5[:, :]), [xj_])
        xl_ = v(lambda e: e.tensor_sub(w2[:, :], g("k00im"), w4[:, :]), [xk_])
        xm_ = v(lambda e: e.tensor_mul(w3[:, :], tan2d[:, :], w2[:, :]), [xl_])
        xn_ = v(lambda e: e.tensor_sub(arr_re[:, :], w1[:, :], w3[:, :]), [xm_])
        xo_ = v(lambda e: e.tensor_mul(w4[:, :], tan2d[:, :], w1[:, :]), [xn_])
        xp_ = v(lambda e: e.tensor_add(arr_im[:, :], w2[:, :], w4[:, :]), [xo_])

        ps_are = self.psum[4][0:64, 0:64]; ps_aim = self.psum[5][0:64, 0:64]
        ma1 = pe(lambda e: e.matmul(ps_are, arr_re[:, :], fc[:, :], start=True,
                                    stop=False), [xn_, c_fc, c_fsn, c_v3, c_b0, c_b1])
        ma2 = pe(lambda e: e.matmul(ps_are, arr_im[:, :], fsn[:, :], start=False,
                                    stop=True), [xp_])
        ma3 = pe(lambda e: e.matmul(ps_aim, arr_re[:, :], fs[:, :], start=True,
                                    stop=False), [])
        ma4 = pe(lambda e: e.matmul(ps_aim, arr_im[:, :], fc[:, :], start=False,
                                    stop=True), [])
        bre2 = w3; bim2 = w4
        u1 = w1; u2 = w2
        y1 = v(lambda e: e.tensor_tensor(u1[:, :], ps_are, tc[:, :], ALU.mult),
               [ma2, c_tc, c_ts])
        y2 = v(lambda e: e.tensor_tensor(u2[:, :], ps_aim, tsw[:, :], ALU.mult), [ma4])
        y3 = v(lambda e: e.tensor_sub(bre2[:, :], u1[:, :], u2[:, :]), [y2])
        y4 = v(lambda e: e.tensor_tensor(u1[:, :], ps_are, tsw[:, :], ALU.mult), [y3])
        y5 = v(lambda e: e.tensor_tensor(u2[:, :], ps_aim, tc[:, :], ALU.mult), [y4])
        y6 = v(lambda e: e.tensor_add(bim2[:, :], u1[:, :], u2[:, :]), [y5])
        ps_km = self.psum[6][0:64, 0:64]
        mk3 = pe(lambda e: e.matmul(ps_km, fc[:, :], bre2[:, :], start=True, stop=False),
                 [y3, y6])
        mk4 = pe(lambda e: e.matmul(ps_km, fsn[:, :], bim2[:, :], start=False, stop=True),
                 [])
        kmat = w5
        zk = v(lambda e: e.tensor_copy(kmat[:, :], ps_km), [mk3, mk4])

        zero64 = tmp64
        z0 = v(lambda e: e.memset(zero64[:, :], 0.0), [c_ts])
        gd = td["g_dram"]
        z1 = em.dma("sync", gd[0:L].rearrange("(a b) -> a b", a=64), zero64[:, :],
                    deps=[z0], inc="gz1")
        z2 = em.dma("sync", gd[L:2 * L].rearrange("(a b) -> a b", a=64), kmat[:, :],
                    deps=[zk], inc="gz2")

        sh32 = td["scratch"]
        sh_toks = []
        for sp in range(128):
            tok = em.dma("sync", sh32[sp:sp + 1, :],
                         gd[L - sp:2 * L - sp].rearrange("(o a) -> o a", o=1),
                         deps=[z1, z2], inc="shld", tag=f"sh{sp}")
            sh_toks.append(tok)
        c_sh = v(lambda e: e.tensor_copy(td["sh"][:, :], sh32[:, :]), sh_toks)
        self.ph0_done = c_sh
        self.sh_toks = sh_toks
        self.tok_onesr = t9

        if self.debug_ph0:
            dk = em.dma("sync", td["k_out"].rearrange("(a b) -> a b", a=64), kmat[:, :],
                        deps=[zk], inc="dbg")
            dsh = em.dma("sync", td["sh_out"][:, :], sh32[:, :], deps=sh_toks,
                         inc="dbg")
            self.dbg_toks = [dk, dsh]
            def dbg(name, shape, src, deps):
                ap = nc.dram_tensor(name, list(shape), F32, kind="ExternalOutput").ap()
                self.dbg_toks.append(em.dma("sync", ap[:, :], src, deps=deps,
                                            inc="dbg", tag=name))
            dbg("d_k8", [8, L], kd[:, :], kst_toks)
            dbg("d_bre", [64, 1], bre[:, :], [c_b0, c_b1])
            dbg("d_bim", [64, 1], bim[:, :], [c_b0, c_b1])
            dbg("d_vcre", [64, 64], vcre[:, :], [c_v0, c_v1])
            dbg("d_vctre", [64, 64], vctre[:, :], [c_v2, c_v3])
            dbg("d_v8a", [64, 8], v8a[:, :], [k4, k5])
            dbg("d_v8b", [64, 8], v8b[:, :], [k4, k5])
            dbg("d_arr_re", [64, 64], arr_re[:, :], [xn_, xp_])
            dbg("d_arr_im", [64, 64], arr_im[:, :], [xn_, xp_])
            dbg("d_tsc", [64, 1], tsc64[:, :], [c_tau])
            dbg("d_bre2", [64, 64], bre2[:, :], [y3, y6])
            dbg("d_bim2", [64, 64], bim2[:, :], [y3, y6])

    def _cast(self, dst, stg, deps, tag, force=None):
        self.cast_flip ^= 1
        use_v = self.cast_flip if force is None else (force == "vector")
        if use_v:
            return self.em.on("vector", lambda e, d=dst, s=stg: e.tensor_copy(d, s),
                              deps=deps, inc="c_v", tag=tag)
        return self.em.on("scalar", lambda e, d=dst, s=stg: e.copy(d, s),
                          deps=deps, inc="c_s", tag=tag)

    def _wq(self):
        return "sync"

    def build_prep(self):
        nc, em, td = self.nc, self.em, self.t
        sb = self.sb
        v = lambda fn, deps, inc="pv": em.on("vector", fn, deps=deps, inc=inc)
        pe = lambda fn, deps, inc="ppe": em.on("tensor", fn, deps=deps, inc=inc)

        onesk = td["onesk"]; onesr = td["onesr"]; dskipt = td["dskipt"]
        ones_bf = sb("ones_bf", [128, 1], BF16)
        ieye32 = sb("ieye32", [128, 128])
        i128bf = sb("i128bf", [128, 128], BF16)
        ihi = sb("ihi", [128, 128], BF16)
        ilo = sb("ilo", [128, 128], BF16)
        s11 = sb("s11", [1, 1]); s128 = sb("s128", [128, 1])
        shi_bf = sb("shi_bf", [128, 1], BF16); shi32 = sb("shi32", [128, 1])
        slo = sb("slo", [128, 1])

        t_eye = em.dma("sync", ieye32[:, :], td["eye128"][:, :], inc="prep_ld")
        c0 = v(lambda e: e.tensor_copy(ones_bf[:, :], onesk[:, :]), [self.tok_onesk])
        c1 = v(lambda e: e.tensor_copy(i128bf[:, :], ieye32[:, :]), [t_eye])
        c2 = v(lambda e: e.tensor_scalar_add(s11[:, :], dskipt[:, :], 1.0),
               [self.tok_dskip])
        ps7 = self.psum[7][:, 0:1]
        m0 = pe(lambda e: e.matmul(ps7, onesr[0:1, :], s11[:, :], start=True,
                                   stop=True), [c2, self.tok_onesr])
        c3 = v(lambda e: e.tensor_copy(s128[:, :], ps7), [m0])
        c4 = v(lambda e: e.tensor_copy(shi_bf[:, :], s128[:, :]), [c3])
        c5 = v(lambda e: e.tensor_copy(shi32[:, :], shi_bf[:, :]), [c4])
        c6 = v(lambda e: e.tensor_sub(slo[:, :], s128[:, :], shi32[:, :]), [c5])
        c7 = v(lambda e: e.tensor_scalar_mul(ihi[:, :], ieye32[:, :], shi32[:, 0:1]),
               [c6])
        c8 = v(lambda e: e.tensor_scalar_mul(ilo[:, :], ieye32[:, :], slo[:, 0:1]),
               [c7])
        self.tok_ident = [c1, c7, c8, c0]
        self.tok_psum7_free = c3

        self.NSW = 8
        self.NSB = 4
        self.wstage_free = [None] * self.NSW
        self.wbuf_free = [None] * self.NSB
        self.ring_idx = 0
        self.cast_flip = 0
        self.dmaq_flip = 0

        stream = td["x1"]; wstage = td["wstage"]
        self.x0tok = []
        for dc in range(NDC):
            toks = []
            for hh in range(2):
                sl = self.ring_idx % self.NSW
                self.ring_idx += 1
                stg = wstage[:, sl * 512:(sl + 1) * 512]
                dtok = em.dma(self._wq(), stg,
                              td["ut"][dc * 128:(dc + 1) * 128,
                                       hh * 512:(hh + 1) * 512],
                              deps=[self.wstage_free[sl]], inc=f"wl{sl}",
                              tag=f"x0 {dc} {hh}")
                dst = stream[:, dc * R_CORE + hh * 512: dc * R_CORE + (hh + 1) * 512]
                ctok = self._cast(dst, stg, [dtok], f"x0c {dc} {hh}")
                self.wstage_free[sl] = ctok
                toks.append(ctok)
            self.x0tok.append(toks)

    def build_ph1(self):
        nc, em, td = self.nc, self.em, self.t
        v = lambda fn, deps, inc="pv": em.on("vector", fn, deps=deps, inc=inc)
        s_ = lambda fn, deps, inc="psc": em.on("scalar", fn, deps=deps, inc=inc)
        pe = lambda fn, deps, inc="ppe": em.on("tensor", fn, deps=deps, inc=inc)

        stream = td["x1"]; hpool = td["hpool"]; sh = td["sh"]; sg = td["sg"]
        scratch = td["scratch"]; small = td["small"]
        ihi, ilo = td["ihi"], td["ilo"]; ones_bf = td["ones_bf"]
        onesr = td["onesr"]

        evac_tok = {}
        x1pre = lambda tc: hpool[:, tc * R_CORE:(tc + 1) * R_CORE]
        prev_stat = None
        pending_stats = []
        for tc in range(NDC):
            par = tc % 2
            psA = self.psum[2 * par]
            psB = self.psum[2 * par + 1]
            deps0 = [self.ph0_done] + self.tok_ident
            for sc in range(tc + 1):
                deps0 += self.x0tok[sc]
            if tc - 2 in evac_tok:
                deps0 += list(evac_tok[tc - 2])
            for sc in range(tc + 1):
                dlt = tc - sc
                lhs = sh[:, dlt * 128:dlt * 128 + 128]
                rhs0 = stream[:, sc * R_CORE: sc * R_CORE + 512]
                rhs1 = stream[:, sc * R_CORE + 512: (sc + 1) * R_CORE]
                st = (sc == 0)
                pe(lambda e, lhs=lhs, r=rhs0, p=psA, st=st:
                   e.matmul(p, lhs, r, start=st, stop=False),
                   deps0 if sc == 0 else [])
                pe(lambda e, lhs=lhs, r=rhs1, p=psB, st=st:
                   e.matmul(p, lhs, r, start=st, stop=False), [])
            rhs0 = stream[:, tc * R_CORE: tc * R_CORE + 512]
            rhs1 = stream[:, tc * R_CORE + 512: (tc + 1) * R_CORE]
            pe(lambda e, r=rhs0, p=psA: e.matmul(p, ihi[:, :], r,
                                                 start=False, stop=False), [])
            pe(lambda e, r=rhs1, p=psB: e.matmul(p, ihi[:, :], r,
                                                 start=False, stop=False), [])
            pe(lambda e, r=rhs0, p=psA: e.matmul(p, ilo[:, :], r,
                                                 start=False, stop=True), [])
            mlast = pe(lambda e, r=rhs1, p=psB: e.matmul(p, ilo[:, :], r,
                                                         start=False, stop=True), [])
            while pending_stats:
                prev_stat = pending_stats.pop(0)()
            xt = x1pre(tc)
            e0 = s_(lambda e, p=psA, xt=xt: e.copy(xt[:, 0:512], p), [mlast])
            e1 = s_(lambda e, p=psB, xt=xt: e.copy(xt[:, 512:1024], p), [])
            sqdep = [e0, e1] + ([prev_stat] if prev_stat else [])
            q0 = s_(lambda e, xt=xt: e.activation(sg[:, 0:512], xt[:, 0:512],
                                                  AF.Square), sqdep)
            q1 = s_(lambda e, xt=xt: e.activation(sg[:, 512:1024], xt[:, 512:1024],
                                                  AF.Square), [])
            evac_tok[tc] = (e0, e1)
            sdep = [e0, e1, q0, q1]
            if tc == 0:
                sdep += [self.ph0_done, self.tok_psum7_free]
            st = (tc == 0); sp = (tc == NDC - 1)

            def emit_stats(xt=xt, st=st, sp=sp, sdep=tuple(sdep)):
                pe(lambda e: e.matmul(self.psum[4][0:1, :], ones_bf[:, :],
                                      xt[:, 0:512], start=st, stop=sp), list(sdep))
                pe(lambda e: e.matmul(self.psum[5][0:1, :], ones_bf[:, :],
                                      xt[:, 512:1024], start=st, stop=sp), [])
                pe(lambda e: e.matmul(self.psum[6][0:1, :], ones_bf[:, :],
                                      sg[:, 0:512], start=st, stop=sp), [])
                return pe(lambda e: e.matmul(self.psum[7][0:1, :], ones_bf[:, :],
                                             sg[:, 512:1024], start=st, stop=sp), [])
            pending_stats.append(emit_stats)

        while pending_stats:
            prev_stat = pending_stats.pop(0)()

        RW = lambda i: scratch[0:1, i * 512:(i + 1) * 512]
        inv = 1.0 / D
        r0 = v(lambda e: e.tensor_scalar_mul(RW(0), self.psum[4][0:1, :], inv),
               [prev_stat, self.ph0_done])
        r1 = v(lambda e: e.tensor_scalar_mul(RW(1), self.psum[5][0:1, :], inv), [r0])
        r2 = v(lambda e: e.tensor_scalar_mul(RW(2), self.psum[6][0:1, :], inv), [r1])
        r3 = v(lambda e: e.tensor_scalar_mul(RW(3), self.psum[7][0:1, :], inv), [r2])
        self.ln1_psum_free = [r0, r1, r2, r3]
        toks_fin = []
        for c in range(2):
            mean, ex2, rstd, nmr = RW(0 + c), RW(2 + c), RW(4 + c), RW(6 + c)
            a0 = v(lambda e, mean=mean, rstd=rstd: e.tensor_mul(rstd, mean, mean),
                   [r3])
            a1 = v(lambda e, ex2=ex2, rstd=rstd, nmr=nmr: e.tensor_sub(nmr, ex2, rstd),
                   [a0])
            a2 = v(lambda e, nmr=nmr, rstd=rstd: e.tensor_scalar_add(rstd, nmr, EPS),
                   [a1])
            a3 = s_(lambda e, rstd=rstd, nmr=nmr: e.activation(nmr, rstd, AF.Ln),
                    [a2])
            a4 = s_(lambda e, nmr=nmr, rstd=rstd: e.activation(rstd, nmr, AF.Exp,
                                                               scale=-0.5), [a3])
            a5 = v(lambda e, mean=mean, rstd=rstd, nmr=nmr: e.tensor_mul(
                nmr, mean, rstd), [a4])
            toks_fin += [a4, a5]
        bc = []
        bdeps = toks_fin + list(evac_tok[NDC - 1]) + list(evac_tok[NDC - 2])
        for i, src_i in enumerate([4, 5, 6, 7]):
            ps = self.psum[i][:, :]
            bc.append(pe(lambda e, ps=ps, src_i=src_i: e.matmul(
                ps, onesr[0:1, :], RW(src_i), start=True, stop=True), bdeps))
        lnb = self.sb("lnb", [128, 2048], BF16)
        bcv = []
        for i in range(4):
            dst = lnb[:, i * 512:(i + 1) * 512]
            ps = self.psum[i][:, :]
            if i % 2 == 0:
                bcv.append(v(lambda e, dst=dst, ps=ps: e.tensor_copy(dst, ps),
                             [bc[i], r3]))
            else:
                bcv.append(s_(lambda e, dst=dst, ps=ps: e.copy(dst, ps),
                              [bc[i], r3]))
        self.ln1_bcast_psum_free = bcv
        rstd_b = lnb[:, 0:1024]
        nmr_b = lnb[:, 1024:2048]
        self.ln1_norm_tok = []
        prev_n1 = None
        tmp_bf = sg
        for tc in range(NDC):
            xt = x1pre(tc)
            dst = stream[:, tc * R_CORE:(tc + 1) * R_CORE]
            n0 = v(lambda e, xt=xt: e.tensor_mul(tmp_bf[:, :], xt, rstd_b),
                   bcv + ([prev_n1] if prev_n1 else []))
            n1 = v(lambda e, dst=dst: e.tensor_sub(dst, tmp_bf[:, :], nmr_b), [n0])
            prev_n1 = n1
            self.ln1_norm_tok.append(n1)
        if self.debug_int:
            x1o = nc.dram_tensor("x1_out", [128, NDC * R_CORE], BF16,
                                 kind="ExternalOutput").ap()
            self.out_toks = getattr(self, "out_toks", []) + [em.dma(
                "sync", x1o[:, :], stream[:, :], deps=[self.ln1_norm_tok[-1]],
                inc="dbg", tag="x1dump")]

    def build_ph2(self, H):
        nc, em, td = self.nc, self.em, self.t
        v = lambda fn, deps, inc="pv": em.on("vector", fn, deps=deps, inc=inc)
        s_ = lambda fn, deps, inc="psc": em.on("scalar", fn, deps=deps, inc=inc)
        pe = lambda fn, deps, inc="ppe": em.on("tensor", fn, deps=deps, inc=inc)
        sb = self.sb

        stream = td["x1"]; hpool = td["hpool"]; sg = td["sg"]
        wstage = td["wstage"]; wbuf = td["wbuf"]
        scratch = td["scratch"]; scratch2 = td["scratch2"]
        i128bf = td["i128bf"]; wgT = td["wgT"]; wdT = td["wdT"]
        small = td["small"]
        lnstat = sb("lnstat", [128, 24])
        stmp = sb("stmp", [128, 8])
        hcol = lambda ic: hpool[:, ic * R_HALF:(ic + 1) * R_HALF]
        xsl = lambda dc: stream[:, dc * R_CORE + H * 512:
                                dc * R_CORE + H * 512 + 512]

        if H == 0:
            h_guard = None
            psum_guard = {0: self.ln1_bcast_psum_free, 1: self.ln1_psum_free}
            stat_guard = []
        else:
            h_guard = [self.h_read_done]
            psum_guard = {0: self.evac2b_last[0], 1: self.evac2b_last[1]}
            stat_guard = [self.ln2_stats_done]

        def xg(dc):
            return [self.ln1_norm_tok[dc]] if H == 0 else []

        def hg(ic):
            if H == 0:
                return [self.ln1_norm_tok[ic // 2]] if ic < 2 * NDC else []
            return h_guard

        z0 = v(lambda e: e.memset(lnstat[:, 0:8], 0.0), stat_guard)

        evac_all = {}
        evac_sg_free = {}
        NOG = NIC // 2
        for og in range(NOG):
            par = og % 2
            pg0, pg1 = self.psum[4 * par], self.psum[4 * par + 1]
            pu0, pu1 = self.psum[4 * par + 2], self.psum[4 * par + 3]
            if og < 2:
                first_deps = list(psum_guard[og])
            else:
                first_deps = list(evac_all[og - 2])
            last_mm = None
            wg2 = wgT.rearrange("d (r o) -> d r o", r=2)
            for dc in range(NDC):
                sl = self.ring_idx % self.NSW
                bl = self.ring_idx % self.NSB
                self.ring_idx += 1
                stg = wstage[:, sl * 512:(sl + 1) * 512]
                wb = wbuf[:, bl * 512:(bl + 1) * 512]
                d0 = em.dma(self._wq(), stg.rearrange("p (r o) -> p r o", r=2),
                            wg2[dc * 128:(dc + 1) * 128, :,
                                og * 256:(og + 1) * 256],
                            deps=[self.wstage_free[sl]], inc=f"wl{sl}",
                            tag=f"2a {H} {og} {dc}")
                ctok = self._cast(wb, stg, [d0, self.wbuf_free[bl]],
                                  f"2a c {H} {og} {dc}",
                                  force=("scalar" if H == 0 else "vector")
                                  if og < 2 else None)
                self.wstage_free[sl] = ctok
                st = (dc == 0); sp = (dc == NDC - 1)
                mdeps = [ctok] + (first_deps if dc == 0 else []) + \
                    (xg(dc) if og == 0 else []) + \
                    (h_guard if (H == 1 and og == 0 and dc == 0) else [])
                pe(lambda e, wb=wb, dc=dc, p=pg0, st=st, sp=sp: e.matmul(
                    p, wb[:, 0:128], xsl(dc), start=st, stop=sp), mdeps)
                pe(lambda e, wb=wb, dc=dc, p=pg1, st=st, sp=sp: e.matmul(
                    p, wb[:, 128:256], xsl(dc), start=st, stop=sp), [])
                pe(lambda e, wb=wb, dc=dc, p=pu0, st=st, sp=sp: e.matmul(
                    p, wb[:, 256:384], xsl(dc), start=st, stop=sp), [])
                last_mm = pe(lambda e, wb=wb, dc=dc, p=pu1, st=st, sp=sp: e.matmul(
                    p, wb[:, 384:512], xsl(dc), start=st, stop=sp), [])
                self.wbuf_free[bl] = last_mm
            ic0, ic1 = 2 * og, 2 * og + 1
            sdep = [last_mm] + (evac_sg_free[og - 1] if og >= 1 else
                                ([self.ln1_norm_tok[-1]] if H == 0 else []))
            s0 = s_(lambda e, p=pg0: e.activation(sg[:, 0:512], p, AF.Silu), sdep)
            s1 = s_(lambda e, p=pg1: e.activation(sg[:, 512:1024], p, AF.Silu), [])
            u0 = v(lambda e, p=pu0, ic0=ic0: e.tensor_tensor(
                hcol(ic0), sg[:, 0:512], p, ALU.mult),
                [s0, last_mm] + (hg(ic0) if og < 2 or H == 0 else []))
            u1 = v(lambda e, p=pu1, ic1=ic1: e.tensor_tensor(
                hcol(ic1), sg[:, 512:1024], p, ALU.mult),
                [s1] + (hg(ic1) if og < 2 or H == 0 else []))
            evac_all[og] = (s0, s1, u0, u1)
            evac_sg_free[og] = [u0, u1]

        t_dram = td["t_dram"]
        evac2b = {}
        evac2b_free = {}
        tguard = getattr(self, "tnorm_load_toks", None)
        tstores = {rt: [] for rt in range(4)}
        h_last_mm = None
        for hp in range(8):
            par = hp % 2
            zt = [self.psum[4 * par + rt] for rt in range(4)]
            if hp == 0:
                first_deps = list(evac_all[NOG - 1])
            elif hp == 1:
                first_deps = list(evac_all[NOG - 2])
            else:
                first_deps = list(evac2b_free[hp - 2])
            for rt in range(4):
                for j in range(4):
                    dsub = 4 * hp + j
                    lhs = stream[:, dsub * R_CORE + H * 512 + rt * 128:
                                 dsub * R_CORE + H * 512 + (rt + 1) * 128]
                    st = (j == 0)
                    dd = first_deps if (rt == 0 and j == 0) else []
                    pe(lambda e, z=zt[rt], rt=rt, j=j, lhs=lhs, st=st: e.matmul(
                        z[:, j * 128:(j + 1) * 128], lhs, i128bf[:, :],
                        start=st, stop=False, skip_group_check=True), dd)
            for ic in range(NIC):
                sl = self.ring_idx % self.NSW
                bl = self.ring_idx % self.NSB
                self.ring_idx += 1
                stg = wstage[:, sl * 512:(sl + 1) * 512]
                wb = wbuf[:, bl * 512:(bl + 1) * 512]
                d0 = em.dma(self._wq(), stg,
                            wdT[ic * 128:(ic + 1) * 128, hp * 512:(hp + 1) * 512],
                            deps=[self.wstage_free[sl]], inc=f"wl{sl}",
                            tag=f"2b {H} {hp} {ic}")
                ctok = self._cast(wb, stg, [d0, self.wbuf_free[bl]],
                                  f"2b c {H} {hp} {ic}")
                self.wstage_free[sl] = ctok
                sp = (ic == NIC - 1)
                for rt in range(4):
                    mm = pe(lambda e, wb=wb, ic=ic, rt=rt, z=zt[rt], sp=sp: e.matmul(
                        z, hcol(ic)[:, rt * 128:(rt + 1) * 128], wb,
                        start=False, stop=sp, skip_group_check=True),
                        [ctok] if rt == 0 else [])
                    if rt == 3:
                        self.wbuf_free[bl] = mm
                        h_last_mm = mm
            etoks = []
            ftoks = []
            for rt in range(4):
                sl2 = rt
                slot = scratch2[:, sl2 * 512:(sl2 + 1) * 512]
                prevf = evac2b_free.get(hp - 1)
                prevd = tstores[rt][-1] if tstores[rt] else None
                wdep = [h_last_mm] + ([prevf[rt]] if prevf else []) + \
                    ([prevd] if prevd else []) + \
                    ([tguard[rt]] if (hp == 0 and tguard) else [])
                c0 = s_(lambda e, slot=slot, z=zt[rt], rt=rt: e.activation(
                    slot, z, AF.Copy, accum_out=stmp[:, rt:rt + 1]), wdep)
                q0 = s_(lambda e, z=zt[rt], rt=rt: e.activation(
                    small[:, 0:512] if rt % 2 == 0 else small[:, 512:1024], z,
                    AF.Square, accum_out=stmp[:, 4 + rt:5 + rt]), [c0])
                a0 = v(lambda e, rt=rt: e.tensor_add(
                    lnstat[:, rt:rt + 1], lnstat[:, rt:rt + 1],
                    stmp[:, rt:rt + 1]), [c0, z0])
                a1 = v(lambda e, rt=rt: e.tensor_add(
                    lnstat[:, 4 + rt:5 + rt], lnstat[:, 4 + rt:5 + rt],
                    stmp[:, 4 + rt:5 + rt]), [q0, a0])
                dout = em.dma("scalar", t_dram[rt * 128:(rt + 1) * 128,
                                               hp * 512:(hp + 1) * 512], slot,
                              deps=[c0], inc=f"ts{rt}", tag=f"t {H} {hp} {rt}")
                tstores[rt].append(dout)
                etoks += [a1, dout]
                ftoks.append(a1)
            evac2b[hp] = etoks
            evac2b_free[hp] = ftoks
        self.evac2b_last = {0: evac2b_free[6], 1: evac2b_free[7]}
        self.h_read_done = h_last_mm
        if self.debug_int and H == 0:
            t0o = nc.dram_tensor("t0_out", [R_HALF, D], F32,
                                 kind="ExternalOutput").ap()
            tdeps = [tstores[rt][-1] for rt in range(4)]
            self.out_toks = getattr(self, "out_toks", []) + [em.dma(
                "sync", t0o[:, :], t_dram[:, :], deps=tdeps, inc="dbg", tag="t0dump")]

        inv = 1.0 / D
        out = td["out"]
        fin_dep = evac2b_free[7] + ([self.last_norm_act] if
                                    getattr(self, "last_norm_act", None) else [])
        self.tnorm_load_toks = []
        self.tpiece_free = [None] * 4
        self.out_toks = getattr(self, "out_toks", [])
        last_b8 = None
        prev_ltok = None
        for rt in range(4):
            mean = lnstat[:, 8 + rt:9 + rt]
            rstd = lnstat[:, 12 + rt:13 + rt]
            nmr = lnstat[:, 16 + rt:17 + rt]
            b0 = v(lambda e, rt=rt, mean=mean: e.tensor_scalar_mul(
                mean, lnstat[:, rt:rt + 1], inv), fin_dep)
            b1 = v(lambda e, rt=rt, rstd=rstd: e.tensor_scalar_mul(
                rstd, lnstat[:, 4 + rt:5 + rt], inv), [b0])
            b2 = v(lambda e, mean=mean, nmr=nmr: e.tensor_mul(nmr, mean, mean), [b1])
            b3 = v(lambda e, rstd=rstd, nmr=nmr: e.tensor_sub(rstd, rstd, nmr), [b2])
            b4 = v(lambda e, rstd=rstd: e.tensor_scalar_add(rstd, rstd, EPS), [b3])
            b5 = s_(lambda e, rstd=rstd, nmr=nmr: e.activation(nmr, rstd, AF.Ln),
                    [b4])
            b6 = s_(lambda e, rstd=rstd, nmr=nmr: e.activation(rstd, nmr, AF.Exp,
                                                               scale=-0.5), [b5])
            b7 = v(lambda e, mean=mean, rstd=rstd, nmr=nmr: e.tensor_mul(
                nmr, mean, rstd), [b6])
            b8 = v(lambda e, nmr=nmr: e.tensor_scalar_mul(nmr, nmr, -1.0), [b7])
            last_b8 = b8
            base_dep = [b8] + tstores[rt] + \
                ([self.ln1_norm_tok[-1]] if H == 0 and rt == 0 else []) + \
                ([self.last_norm_act] if (H == 1 and rt == 0 and
                                          getattr(self, "last_norm_act", None))
                 else [])
            piece_toks = []
            for q in range(4):
                pd = list(base_dep) + \
                    ([self.tpiece_free[q]] if self.tpiece_free[q] else [])
                ptok = em.dma("scalar", scratch[:, q * 1024:(q + 1) * 1024],
                              t_dram[rt * 128:(rt + 1) * 128,
                                     q * 1024:(q + 1) * 1024],
                              deps=pd, inc=f"tp{q}", tag=f"tl {H} {rt} {q}")
                piece_toks.append(ptok)
            self.tnorm_load_toks += piece_toks
            prev_out = [None] * 2
            last_n0 = None
            for cc in range(4):
                sl2 = cc % 2
                slot = scratch2[:, sl2 * 1024:(sl2 + 1) * 1024]
                ndep = [piece_toks[cc], b8] + \
                    ([prev_out[sl2]] if prev_out[sl2] else
                     ([tstores[2 * sl2][-1], tstores[2 * sl2 + 1][-1]]
                      if rt == 0 else []))
                n0 = v(lambda e, slot=slot, cc=cc, rstd=rstd, nmr=nmr:
                       e.tensor_scalar(slot,
                                       scratch[:, cc * 1024:(cc + 1) * 1024],
                                       rstd, nmr, ALU.mult, ALU.add), ndep)
                o0 = em.dma("scalar",
                            out[H * 512 + rt * 128: H * 512 + (rt + 1) * 128,
                                cc * 1024:(cc + 1) * 1024], slot,
                            deps=[n0], inc=f"os{sl2}", tag=f"o {H} {rt} {cc}")
                prev_out[sl2] = o0
                self.tpiece_free[cc] = n0
                last_n0 = n0
                self.out_toks.append(o0)
            prev_ltok = last_n0
            self.last_norm_act = last_n0
        self.ln2_stats_done = last_b8

    def finish(self, extra_deps=()):
        em = self.em
        deps = list(extra_deps) + list(getattr(self, "out_toks", []))
        if self.debug_ph0:
            deps += self.dbg_toks
        def fin(e):
            return None
        em.on("sync", fin, deps=deps, tag="final")
        with self.nc.Block() as block:
            em.finalize(block)
        return self.nc


def host_inputs_common(inp):
    d = {
        "wgT": np.ascontiguousarray(inp["L0_ffn_w_gate_up"].T),
        "wdT": np.ascontiguousarray(inp["L0_ffn_w_down"].T),
        "lam": np.asarray(inp["L0_attn_Lambda_ri"]),
        "pri": np.asarray(inp["L0_attn_p_ri"]),
        "qri": np.asarray(inp["L0_attn_q_ri"]),
        "vc": np.asarray(inp["L0_attn_Vc_ri"]),
        "ct": np.asarray(inp["L0_attn_Ct_ri"]),
        "bp": np.asarray(inp["L0_attn_Bp"]),
        "dskip": np.asarray(inp["L0_attn_D"]),
        "lstep": np.asarray(inp["L0_attn_log_step"]),
    }
    d.update(make_constants())
    return d


_NC_CACHE = {}


def build_full():
    if "full" not in _NC_CACHE:
        b = Builder()
        b.build_ph0()
        b.build_prep()
        b.build_ph1()
        b.build_ph2(0)
        b.build_ph2(1)
        _NC_CACHE["full"] = b.finish()
    return _NC_CACHE["full"]


def kernel(**inputs):
    nc = build_full()
    common = host_inputs_common(inputs)
    u = np.asarray(inputs["L0_attn_u"])
    uT = np.ascontiguousarray(u.reshape(8192, 4096).T)
    in_maps = []
    for c in range(N_CORES):
        m = dict(common)
        m["ut"] = np.ascontiguousarray(uT[:, c * R_CORE:(c + 1) * R_CORE])
        in_maps.append(m)
    res = run_bass_kernel_spmd(nc, in_maps, core_ids=list(range(N_CORES)))
    outs = [res.results[c]["out"] for c in range(N_CORES)]
    full = np.concatenate(outs, axis=0).reshape(1, 8192, 4096).astype(np.float32)
    return full


# version 15
# speedup vs baseline: 1.1129x; 1.1129x over previous
import math
import numpy as np
import ml_dtypes
import concourse.bass as bass
import concourse.mybir as mybir
from concourse.bass_utils import run_bass_kernel_spmd

F32 = mybir.dt.float32
BF16 = mybir.dt.bfloat16
AF = mybir.ActivationFunctionType
ALU = mybir.AluOpType
AX = mybir.AxisListType

N_CORES = 8
D = 4096
R_CORE = 1024
R_HALF = 512
I_FFN = 11008
NIC = I_FFN // 128
NDC = D // 128
L = 4096
EPS = 1e-5


def _hilo(x64):
    hi = x64.astype(np.float32)
    lo = (x64 - hi.astype(np.float64)).astype(np.float32)
    return hi, lo


def make_constants():
    c = {}
    j = np.arange(L, dtype=np.float64)
    tanv = np.tan(np.pi * j / L)
    c["tan_hi"], c["tan_lo"] = _hilo(tanv.reshape(1, L))
    c["tan2d_hi"], c["tan2d_lo"] = _hilo(tanv.reshape(64, 64))
    a = np.arange(64, dtype=np.float64)
    ang64 = 2.0 * np.pi * np.outer(a, a) / 64.0
    c["fc_hi"], c["fc_lo"] = _hilo(np.cos(ang64))
    c["fs_hi"], c["fs_lo"] = _hilo(np.sin(ang64))
    angT = 2.0 * np.pi * np.outer(a, a) / 4096.0
    c["tc_hi"], c["tc_lo"] = _hilo(np.cos(angT) / 4096.0)
    c["ts_hi"], c["ts_lo"] = _hilo(np.sin(angT) / 4096.0)
    c["eye64"] = np.eye(64, dtype=np.float32)
    c["eye128"] = np.eye(128, dtype=np.float32)
    c["ones_k"] = np.ones((128, 1), dtype=np.float32)
    c["ones_r"] = np.ones((1, 128), dtype=np.float32)
    return c


ENGINES = ("sync", "gpsimd", "scalar", "tensor", "vector")


class Em:

    def __init__(self, nc):
        self.nc = nc
        self.sems = {}
        self.counts = {}
        self.events = {e: [] for e in ENGINES}
        self.last_wait = {e: {} for e in ENGINES}
        self.same_engine_sync = True

    def sem(self, name):
        if name not in self.sems:
            self.sems[name] = self.nc.alloc_semaphore(name)
            self.counts[name] = 0
        return self.sems[name]

    def on(self, engine, fn, deps=(), inc=None, inc_amt=1, tag=""):
        waits = {}
        for t in deps:
            if t is None:
                continue
            s, v, src = t
            if src == engine and not self.same_engine_sync:
                continue
            if self.last_wait[engine].get(s, -1) >= v:
                continue
            if waits.get(s, -1) < v:
                waits[s] = v
        for s, v in waits.items():
            self.last_wait[engine][s] = v
        tok = None
        if inc is not None:
            self.sem(inc)
            self.counts[inc] += inc_amt
            tok = (inc, self.counts[inc], engine)
        self.events[engine].append((fn, sorted(waits.items()), inc, inc_amt, tag))
        return tok

    def dma(self, queue, out, in_, deps=(), inc=None, tag=""):
        def fn(eng, out=out, in_=in_):
            return eng.dma_start(out=out, in_=in_)
        return self.on(queue, fn, deps=deps, inc=inc, inc_amt=16, tag=tag)

    def emit(self, engine_obj, engine_name):
        for fn, waits, inc, inc_amt, tag in self.events[engine_name]:
            for s, v in waits:
                engine_obj.wait_ge(self.sems[s], v)
            ins = fn(engine_obj)
            if inc is not None:
                if ins is None:
                    ins = engine_obj.nop()
                ins.then_inc(self.sems[inc], inc_amt)

    def finalize(self, block):
        for name in ENGINES:
            if not self.events[name]:
                continue
            getattr(block, name)(
                lambda eng, _n=name: self.emit(eng, _n)
            )


class Builder:
    def __init__(self, debug_ph0=False, lite=False, debug_int=False):
        nc = bass.Bass("TRN2", target_bir_lowering=False, dynamic_dma_scratch_size=512)
        self.nc = nc
        self.debug_ph0 = debug_ph0
        self.debug_int = debug_int
        self.lite = lite
        self.em = Em(nc)
        self.t = {}

        td = self.t
        def din(name, shape):
            td[name] = nc.dram_tensor(name, list(shape), F32, kind="ExternalInput").ap()
        if not lite:
            din("ut", (D, R_CORE))
            din("wgT", (D, 2 * I_FFN))
            din("wdT", (I_FFN, D))
        din("lam", (64, 2))
        din("pri", (64, 2))
        din("qri", (64, 2))
        din("vc", (64, 64, 2))
        din("ct", (1, 64, 2))
        din("bp", (64, 1))
        din("dskip", (1,))
        din("lstep", (1,))
        for k, v in make_constants().items():
            din(k, v.shape)

        td["out"] = nc.dram_tensor("out", [R_CORE, D], F32, kind="ExternalOutput").ap()
        td["g_dram"] = nc.dram_tensor("g_dram", [2 * L], F32, kind="Internal").ap()
        td["k_dram"] = nc.dram_tensor("k_dram", [8, L], F32, kind="Internal").ap()
        td["t_dram"] = nc.dram_tensor("t_dram", [R_HALF, D], F32, kind="Internal").ap()
        if debug_ph0:
            td["k_out"] = nc.dram_tensor("k_out", [L], F32, kind="ExternalOutput").ap()
            td["sh_out"] = nc.dram_tensor("sh_out", [128, L], F32, kind="ExternalOutput").ap()

        def sb(name, shape, dtype):
            td[name] = nc.alloc_sbuf_tensor(name, list(shape), dtype).ap()
            return td[name]

        sb("x1", [128, NDC * R_CORE], BF16)
        sb("hpool", [128, NIC * R_HALF], BF16)
        sb("sh", [128, L], BF16)
        sb("wstage", [128, 8 * 512], F32)
        sb("wbuf", [128, 4 * 512], BF16)
        sb("sg", [128, 2 * 512], BF16)
        sb("scratch", [128, 4096], F32)
        sb("scratch2", [128, 2048], F32)
        sb("small", [128, 1024], F32)

        self.psum = [nc.alloc_psum_tensor(f"pb{i}", [128, 512], F32).ap()
                     for i in range(8)]

    def sb(self, name, shape, dtype=F32):
        if name not in self.t:
            self.t[name] = self.nc.alloc_sbuf_tensor(name, list(shape), dtype).ap()
        return self.t[name]

    def build_ph0(self):
        nc, em, td = self.nc, self.em, self.t
        sb = self.sb

        v = lambda fn, deps, inc="ph0_v": em.on("vector", fn, deps=deps, inc=inc)
        s_ = lambda fn, deps, inc="ph0_s": em.on("scalar", fn, deps=deps, inc=inc)
        pe = lambda fn, deps, inc="ph0_pe": em.on("tensor", fn, deps=deps, inc=inc)
        self._plid = 0
        def ld(dst, src, deps=(), tag=""):
            self._plid += 1
            return em.dma("sync", dst, src, deps=deps,
                          inc=f"pl{self._plid}", tag=tag)

        lamt = sb("lamt", [64, 2]); prit = sb("prit", [64, 2]); qrit = sb("qrit", [64, 2])
        ctt = sb("ctt", [64, 2]); bpt = sb("bpt", [64, 1]); lstept = sb("lstept", [1, 1])
        dskipt = sb("dskipt", [1, 1])
        vct = sb("vct", [64, 128])
        eye64t = sb("eye64t", [64, 64])
        onesk = sb("onesk", [128, 1]); onesr = sb("onesr", [1, 128])
        fc = sb("fc", [64, 64]); fs = sb("fs", [64, 64]); fsn = sb("fsn", [64, 64])
        tc = sb("tc", [64, 64]); tsw = sb("tsw", [64, 64])
        tan2d = sb("tan2d", [64, 64])
        tmp64 = sb("tmp64", [64, 64])
        tmp64b = sb("tmp64b", [64, 64])
        tansum = td["scratch"][0:1, 0:L]
        tanlo_row = td["scratch"][0:1, 0:L]

        t0 = ld(lamt[:, :], td["lam"][:, :], tag="lam")
        t1 = ld(prit[:, :], td["pri"][:, :], tag="pri")
        t2 = ld(qrit[:, :], td["qri"][:, :], tag="qri")
        t3 = ld(ctt[:, :], td["ct"][0, :, :], tag="ct")
        t4 = ld(bpt[:, :], td["bp"][:, :], tag="bp")
        t5 = ld(lstept[:, :], td["lstep"].rearrange("(o a) -> o a", o=1), tag="lstep")
        self.tok_dskip = ld(dskipt[:, :], td["dskip"].rearrange("(o a) -> o a", o=1), tag="dskip")
        t6 = ld(vct[:, :], td["vc"].rearrange("a b c -> a (b c)"), tag="vc")
        t7 = ld(eye64t[:, :], td["eye64"][:, :], tag="eye64")
        self.tok_onesk = ld(onesk[:, :], td["ones_k"][:, :], tag="onesk")
        t9 = ld(onesr[:, :], td["ones_r"][:, :], tag="onesr")
        tanlo_tmp = td["scratch2"][0:1, 0:2048]
        t10 = ld(tansum, td["tan_hi"][:, :], tag="tanhi")
        t11a = ld(tanlo_tmp, td["tan_lo"][:, 0:2048], tag="tanlo_a")
        t12 = ld(fc[:, :], td["fc_hi"][:, :], tag="fchi")
        t13 = ld(tmp64[:, :], td["fc_lo"][:, :], tag="fclo")
        t20 = ld(tan2d[:, :], td["tan2d_hi"][:, :], tag="t2dhi")
        t21 = ld(tmp64b[:, :], td["tan2d_lo"][:, :], tag="t2dlo")

        c_tan_a = v(lambda e: e.tensor_add(tansum[:, 0:2048], tansum[:, 0:2048],
                                           tanlo_tmp), [t10, t11a])
        t11b = ld(tanlo_tmp, td["tan_lo"][:, 2048:4096], deps=[c_tan_a], tag="tanlo_b")
        c_tan = v(lambda e: e.tensor_add(tansum[:, 2048:4096], tansum[:, 2048:4096],
                                         tanlo_tmp), [t11b])
        c_fc = v(lambda e: e.tensor_add(fc[:, :], fc[:, :], tmp64[:, :]), [t12, t13])
        c_t2d = v(lambda e: e.tensor_add(tan2d[:, :], tan2d[:, :], tmp64b[:, :]), [t20, t21])
        t14 = ld(fs[:, :], td["fs_hi"][:, :], tag="fshi")
        t15 = ld(tmp64[:, :], td["fs_lo"][:, :], deps=[c_fc], tag="fslo")
        c_fs = v(lambda e: e.tensor_add(fs[:, :], fs[:, :], tmp64[:, :]), [t14, t15])
        c_fsn = v(lambda e: e.tensor_scalar_mul(fsn[:, :], fs[:, :], -1.0), [c_fs])
        t16 = ld(tc[:, :], td["tc_hi"][:, :], tag="tchi")
        t17 = ld(tmp64[:, :], td["tc_lo"][:, :], deps=[c_fs], tag="tclo")
        c_tc = v(lambda e: e.tensor_add(tc[:, :], tc[:, :], tmp64[:, :]), [t16, t17])
        t18 = ld(tsw[:, :], td["ts_hi"][:, :], tag="tshi")
        t19 = ld(tmp64[:, :], td["ts_lo"][:, :], deps=[c_tc], tag="tslo")
        c_ts = v(lambda e: e.tensor_add(tsw[:, :], tsw[:, :], tmp64[:, :]), [t18, t19])

        tscale = sb("tscale", [1, 1])
        c_sc = s_(lambda e: e.activation(tscale[:, :], lstept[:, :], AF.Exp,
                                         scale=-1.0), [t5])
        tsc64 = sb("tsc64", [64, 1])
        ps_sc = self.psum[6][0:64, 0:1]
        m_sc = pe(lambda e: e.matmul(ps_sc, onesr[0:1, 0:64], tscale[:, :],
                                     start=True, stop=True), [c_sc, t9])
        c_tau = v(lambda e: e.tensor_scalar_mul(tsc64[:, :], ps_sc, 2.0), [m_sc])

        lre = sb("lre", [64, 1]); lim = sb("lim", [64, 1])
        neg_lre = sb("neg_lre", [64, 1]); lre2 = sb("lre2", [64, 1])
        c_l0 = v(lambda e: e.tensor_copy(lre[:, :], lamt[:, 0:1]), [t0])
        c_l1 = v(lambda e: e.tensor_copy(lim[:, :], lamt[:, 1:2]), [t0])
        c_l2 = v(lambda e: e.tensor_scalar_mul(neg_lre[:, :], lre[:, :], -1.0), [c_l0])
        c_l3 = v(lambda e: e.tensor_mul(lre2[:, :], lre[:, :], lre[:, :]), [c_l0])

        vcre = sb("vcre", [64, 64]); vcim = sb("vcim", [64, 64])
        c_v0 = v(lambda e: e.tensor_copy(vcre[:, :], vct[:, 0:128:2]), [t6])
        c_v1 = v(lambda e: e.tensor_copy(vcim[:, :], vct[:, 1:128:2]), [t6])
        ps_t = self.psum[4][0:64, 0:64]
        vctre = sb("vctre", [64, 64]); vctim = sb("vctim", [64, 64])
        m_t0 = pe(lambda e: e.transpose(ps_t, vcre[:, :], eye64t[:, :]), [c_v0, t7],
                  inc="ph0_pe")
        c_v2 = v(lambda e: e.tensor_copy(vctre[:, :], ps_t), [m_t0])
        m_t1 = pe(lambda e: e.transpose(ps_t, vcim[:, :], eye64t[:, :]), [c_v1, c_v2])
        c_v3 = v(lambda e: e.tensor_copy(vctim[:, :], ps_t), [m_t1])
        ps_b = self.psum[5][0:64, 0:2]
        m_b0 = pe(lambda e: e.matmul(ps_b[:, 0:1], vctre[:, :], bpt[:, :],
                                     start=True, stop=True), [c_v2, t4])
        m_b1 = pe(lambda e: e.matmul(ps_b[:, 1:2], vctim[:, :], bpt[:, :],
                                     start=True, stop=True), [c_v3])
        bre = sb("bre", [64, 1]); bim = sb("bim", [64, 1])
        c_b0 = v(lambda e: e.tensor_copy(bre[:, :], ps_b[:, 0:1]), [m_b0])
        c_b1 = v(lambda e: e.tensor_copy(bim[:, :], ps_b[:, 1:2]), [m_b1])

        v8a = sb("v8a", [64, 8]); v8b = sb("v8b", [64, 8])
        ctre, ctim = ctt[:, 0:1], ctt[:, 1:2]
        pre_, pim_ = prit[:, 0:1], prit[:, 1:2]
        qre_, qim_ = qrit[:, 0:1], qrit[:, 1:2]
        tmpa = sb("tmpa", [64, 1]); tmpb = sb("tmpb", [64, 1])
        vdeps = [t1, t2, t3, c_b0, c_b1]

        def cprod(col, are, aim, brr, bii, dep_extra):
            d = vdeps + dep_extra
            tk1 = v(lambda e: e.tensor_mul(tmpa[:, :], are, brr), d)
            tk2 = v(lambda e: e.tensor_mul(tmpb[:, :], aim, bii), [tk1])
            tk3 = v(lambda e: e.tensor_add(v8a[:, col:col + 1], tmpa[:, :], tmpb[:, :]),
                    [tk2])
            tk4 = v(lambda e: e.tensor_mul(tmpa[:, :], are, bii), [tk3])
            tk5 = v(lambda e: e.tensor_mul(tmpb[:, :], aim, brr), [tk4])
            tk6 = v(lambda e: e.tensor_sub(v8a[:, col + 4:col + 5], tmpa[:, :], tmpb[:, :]),
                    [tk5])
            return tk6
        k0 = cprod(0, ctre, ctim, bre[:, :], bim[:, :], [])
        k1 = cprod(1, ctre, ctim, pre_, pim_, [k0])
        k2 = cprod(2, qre_, qim_, bre[:, :], bim[:, :], [k1])
        k3 = cprod(3, qre_, qim_, pre_, pim_, [k2])
        k4 = v(lambda e: e.tensor_copy(v8b[:, 0:4], v8a[:, 4:8]), [k3])
        k5 = v(lambda e: e.tensor_scalar_mul(v8b[:, 4:8], v8a[:, 0:4], -1.0), [k3])

        ps_tb = [self.psum[0][0:64, :], self.psum[1][0:64, :]]
        ps_k = [self.psum[2][0:8, :], self.psum[3][0:8, :]]
        s2 = td["scratch2"]
        r_dim = s2[0:64, 0:512]; r_recip = s2[0:64, 512:1024]
        r_m2 = s2[0:64, 1024:1536]; r_rimn = s2[0:64, 1536:2048]
        dim_sb = r_dim; m2a = r_m2; recip = r_recip
        rre = r_m2; rimn = r_rimn
        ktmp0 = td["small"][0:8, 0:512]
        ktmp = [ktmp0, ktmp0]
        kd = td["k_dram"]
        NCH = L // 512
        prev_done = [None, None]
        prev_dve = [k4, k5, c_tau, c_l1, c_l2, c_l3]
        prev_pe_read = [None, None]
        for ch in range(NCH):
            sl = ch % 2
            pb = ps_tb[sl]
            pk = ps_k[sl]
            jlo = ch * 512
            mtau = pe(lambda e, pb=pb, jlo=jlo: e.matmul(
                pb, onesr[0:1, 0:64], tansum[:, jlo:jlo + 512],
                start=True, stop=True),
                [c_tan, t9] + ([prev_pe_read[sl]] if prev_pe_read[sl] else []))
            d1 = v(lambda e, pb=pb: e.tensor_scalar(
                dim_sb, pb, tsc64[0:64, 0:1], lim[0:64, 0:1],
                ALU.mult, ALU.subtract),
                [mtau, c_tau] + prev_dve)
            d2 = v(lambda e: e.tensor_mul(m2a, dim_sb, dim_sb), [d1])
            d3 = v(lambda e: e.tensor_scalar(r_rimn, m2a, lre2[0:64, 0:1], None,
                                             ALU.add), [d2])
            d4 = v(lambda e: e.reciprocal(recip, r_rimn), [d3])
            d5 = v(lambda e: e.tensor_scalar(rre, recip, neg_lre[0:64, 0:1], None,
                                             ALU.mult), [d4])
            d6 = v(lambda e: e.tensor_mul(rimn, dim_sb, recip), [d5])
            prev_pe_read[sl] = d1
            mk1 = pe(lambda e, pk=pk: e.matmul(pk, v8a[0:64, :], rre, start=True,
                                               stop=False), [d5, d6])
            mk2 = pe(lambda e, pk=pk: e.matmul(pk, v8b[0:64, :], rimn, start=False,
                                               stop=True), [])
            ev = s_(lambda e, pk=pk, sl=sl: e.copy(ktmp[sl][:, :], pk),
                    [mk2] + ([prev_done[sl]] if prev_done[sl] else []))
            st = em.dma("sync", kd[0:8, jlo:jlo + 512], ktmp[sl][:, :], deps=[ev],
                        inc="kst", tag=f"kst{ch}")
            prev_done[0] = prev_done[1] = st
            prev_dve = [d6]
        kst_toks = list(prev_done)

        names = ["k00re", "k01re", "k10re", "k11re", "k00im", "k01im", "k10im", "k11im"]
        kt = {}
        ktoks = []
        for i, nm in enumerate(names):
            tile = sb("kt_" + nm, [64, 64])
            tok = ld(tile[:, :], kd[i, :].rearrange("(a b) -> a b", a=64),
                     deps=kst_toks, tag="kld" + nm)
            kt[nm] = tile
            ktoks.append(tok)

        if self.debug_ph0:
            w1 = sb("w1", [64, 64]); w2 = sb("w2", [64, 64])
            w3 = sb("w3", [64, 64]); w4 = sb("w4", [64, 64])
        else:
            w1 = vcre; w2 = vcim; w3 = vctre; w4 = vctim
        w5 = sb("w5", [64, 64])
        arr_re = sb("arr_re", [64, 64]); arr_im = sb("arr_im", [64, 64])
        g = lambda nm: kt[nm][:, :]
        deps0 = ktoks + [c_t2d]
        x1_ = v(lambda e: e.tensor_scalar_add(w1[:, :], g("k11re"), 1.0), deps0)
        x2_ = v(lambda e: e.tensor_mul(w2[:, :], w1[:, :], w1[:, :]), [x1_])
        x3_ = v(lambda e: e.tensor_mul(w3[:, :], g("k11im"), g("k11im")), [x2_])
        x4_ = v(lambda e: e.tensor_add(w2[:, :], w2[:, :], w3[:, :]), [x3_])
        x5_ = v(lambda e: e.reciprocal(w2[:, :], w2[:, :]), [x4_])
        x6_ = v(lambda e: e.tensor_mul(w3[:, :], g("k01re"), g("k10re")), [x5_])
        x7_ = v(lambda e: e.tensor_mul(w4[:, :], g("k01im"), g("k10im")), [x6_])
        x8_ = v(lambda e: e.tensor_sub(w3[:, :], w3[:, :], w4[:, :]), [x7_])
        x9_ = v(lambda e: e.tensor_mul(w4[:, :], g("k01re"), g("k10im")), [x8_])
        xa_ = v(lambda e: e.tensor_mul(w5[:, :], g("k01im"), g("k10re")), [x9_])
        xb_ = v(lambda e: e.tensor_add(w4[:, :], w4[:, :], w5[:, :]), [xa_])
        xc_ = v(lambda e: e.tensor_mul(w5[:, :], w3[:, :], w1[:, :]), [xb_])
        xd_ = v(lambda e: e.tensor_mul(arr_re[:, :], w4[:, :], g("k11im")), [xc_])
        xe_ = v(lambda e: e.tensor_add(w5[:, :], w5[:, :], arr_re[:, :]), [xd_])
        xf_ = v(lambda e: e.tensor_mul(w5[:, :], w5[:, :], w2[:, :]), [xe_])
        xg_ = v(lambda e: e.tensor_mul(w4[:, :], w4[:, :], w1[:, :]), [xf_])
        xh_ = v(lambda e: e.tensor_mul(w3[:, :], w3[:, :], g("k11im")), [xg_])
        xi_ = v(lambda e: e.tensor_sub(w4[:, :], w4[:, :], w3[:, :]), [xh_])
        xj_ = v(lambda e: e.tensor_mul(w4[:, :], w4[:, :], w2[:, :]), [xi_])
        xk_ = v(lambda e: e.tensor_sub(w1[:, :], g("k00re"), w5[:, :]), [xj_])
        xl_ = v(lambda e: e.tensor_sub(w2[:, :], g("k00im"), w4[:, :]), [xk_])
        xm_ = v(lambda e: e.tensor_mul(w3[:, :], tan2d[:, :], w2[:, :]), [xl_])
        xn_ = v(lambda e: e.tensor_sub(arr_re[:, :], w1[:, :], w3[:, :]), [xm_])
        xo_ = v(lambda e: e.tensor_mul(w4[:, :], tan2d[:, :], w1[:, :]), [xn_])
        xp_ = v(lambda e: e.tensor_add(arr_im[:, :], w2[:, :], w4[:, :]), [xo_])

        ps_are = self.psum[4][0:64, 0:64]; ps_aim = self.psum[5][0:64, 0:64]
        ma1 = pe(lambda e: e.matmul(ps_are, arr_re[:, :], fc[:, :], start=True,
                                    stop=False), [xn_, c_fc, c_fsn, c_v3, c_b0, c_b1])
        ma2 = pe(lambda e: e.matmul(ps_are, arr_im[:, :], fsn[:, :], start=False,
                                    stop=True), [xp_])
        ma3 = pe(lambda e: e.matmul(ps_aim, arr_re[:, :], fs[:, :], start=True,
                                    stop=False), [])
        ma4 = pe(lambda e: e.matmul(ps_aim, arr_im[:, :], fc[:, :], start=False,
                                    stop=True), [])
        bre2 = w3; bim2 = w4
        u1 = w1; u2 = w2
        y1 = v(lambda e: e.tensor_tensor(u1[:, :], ps_are, tc[:, :], ALU.mult),
               [ma2, c_tc, c_ts])
        y2 = v(lambda e: e.tensor_tensor(u2[:, :], ps_aim, tsw[:, :], ALU.mult), [ma4])
        y3 = v(lambda e: e.tensor_sub(bre2[:, :], u1[:, :], u2[:, :]), [y2])
        y4 = v(lambda e: e.tensor_tensor(u1[:, :], ps_are, tsw[:, :], ALU.mult), [y3])
        y5 = v(lambda e: e.tensor_tensor(u2[:, :], ps_aim, tc[:, :], ALU.mult), [y4])
        y6 = v(lambda e: e.tensor_add(bim2[:, :], u1[:, :], u2[:, :]), [y5])
        ps_km = self.psum[6][0:64, 0:64]
        mk3 = pe(lambda e: e.matmul(ps_km, fc[:, :], bre2[:, :], start=True, stop=False),
                 [y3, y6])
        mk4 = pe(lambda e: e.matmul(ps_km, fsn[:, :], bim2[:, :], start=False, stop=True),
                 [])
        kmat = w5
        zk = v(lambda e: e.tensor_copy(kmat[:, :], ps_km), [mk3, mk4])

        zero64 = tmp64
        z0 = v(lambda e: e.memset(zero64[:, :], 0.0), [c_ts])
        gd = td["g_dram"]
        z1 = em.dma("sync", gd[0:L].rearrange("(a b) -> a b", a=64), zero64[:, :],
                    deps=[z0], inc="gz1")
        z2 = em.dma("sync", gd[L:2 * L].rearrange("(a b) -> a b", a=64), kmat[:, :],
                    deps=[zk], inc="gz2")

        sh32 = td["scratch"]
        sh_toks = []
        for sp in range(128):
            tok = em.dma("sync", sh32[sp:sp + 1, :],
                         gd[L - sp:2 * L - sp].rearrange("(o a) -> o a", o=1),
                         deps=[z1, z2], inc="shld", tag=f"sh{sp}")
            sh_toks.append(tok)
        c_sh = v(lambda e: e.tensor_copy(td["sh"][:, :], sh32[:, :]), sh_toks)
        self.ph0_done = c_sh
        self.sh_toks = sh_toks
        self.tok_onesr = t9

        if self.debug_ph0:
            dk = em.dma("sync", td["k_out"].rearrange("(a b) -> a b", a=64), kmat[:, :],
                        deps=[zk], inc="dbg")
            dsh = em.dma("sync", td["sh_out"][:, :], sh32[:, :], deps=sh_toks,
                         inc="dbg")
            self.dbg_toks = [dk, dsh]
            def dbg(name, shape, src, deps):
                ap = nc.dram_tensor(name, list(shape), F32, kind="ExternalOutput").ap()
                self.dbg_toks.append(em.dma("sync", ap[:, :], src, deps=deps,
                                            inc="dbg", tag=name))
            dbg("d_k8", [8, L], kd[:, :], kst_toks)
            dbg("d_bre", [64, 1], bre[:, :], [c_b0, c_b1])
            dbg("d_bim", [64, 1], bim[:, :], [c_b0, c_b1])
            dbg("d_vcre", [64, 64], vcre[:, :], [c_v0, c_v1])
            dbg("d_vctre", [64, 64], vctre[:, :], [c_v2, c_v3])
            dbg("d_v8a", [64, 8], v8a[:, :], [k4, k5])
            dbg("d_v8b", [64, 8], v8b[:, :], [k4, k5])
            dbg("d_arr_re", [64, 64], arr_re[:, :], [xn_, xp_])
            dbg("d_arr_im", [64, 64], arr_im[:, :], [xn_, xp_])
            dbg("d_tsc", [64, 1], tsc64[:, :], [c_tau])
            dbg("d_bre2", [64, 64], bre2[:, :], [y3, y6])
            dbg("d_bim2", [64, 64], bim2[:, :], [y3, y6])

    def _cast(self, dst, stg, deps, tag, force=None):
        self.cast_flip ^= 1
        use_v = self.cast_flip if force is None else (force == "vector")
        if use_v:
            return self.em.on("vector", lambda e, d=dst, s=stg: e.tensor_copy(d, s),
                              deps=deps, inc="c_v", tag=tag)
        return self.em.on("scalar", lambda e, d=dst, s=stg: e.copy(d, s),
                          deps=deps, inc="c_s", tag=tag)

    def _wq(self):
        return "sync"

    def build_prep(self):
        nc, em, td = self.nc, self.em, self.t
        sb = self.sb
        v = lambda fn, deps, inc="pv": em.on("vector", fn, deps=deps, inc=inc)
        pe = lambda fn, deps, inc="ppe": em.on("tensor", fn, deps=deps, inc=inc)

        onesk = td["onesk"]; onesr = td["onesr"]; dskipt = td["dskipt"]
        ones_bf = sb("ones_bf", [128, 1], BF16)
        ieye32 = sb("ieye32", [128, 128])
        i128bf = sb("i128bf", [128, 128], BF16)
        ihi = sb("ihi", [128, 128], BF16)
        ilo = sb("ilo", [128, 128], BF16)
        s11 = sb("s11", [1, 1]); s128 = sb("s128", [128, 1])
        shi_bf = sb("shi_bf", [128, 1], BF16); shi32 = sb("shi32", [128, 1])
        slo = sb("slo", [128, 1])

        t_eye = em.dma("sync", ieye32[:, :], td["eye128"][:, :], inc="prep_ld")
        c0 = v(lambda e: e.tensor_copy(ones_bf[:, :], onesk[:, :]), [self.tok_onesk])
        c1 = v(lambda e: e.tensor_copy(i128bf[:, :], ieye32[:, :]), [t_eye])
        c2 = v(lambda e: e.tensor_scalar_add(s11[:, :], dskipt[:, :], 1.0),
               [self.tok_dskip])
        ps7 = self.psum[7][:, 0:1]
        m0 = pe(lambda e: e.matmul(ps7, onesr[0:1, :], s11[:, :], start=True,
                                   stop=True), [c2, self.tok_onesr])
        c3 = v(lambda e: e.tensor_copy(s128[:, :], ps7), [m0])
        c4 = v(lambda e: e.tensor_copy(shi_bf[:, :], s128[:, :]), [c3])
        c5 = v(lambda e: e.tensor_copy(shi32[:, :], shi_bf[:, :]), [c4])
        c6 = v(lambda e: e.tensor_sub(slo[:, :], s128[:, :], shi32[:, :]), [c5])
        c7 = v(lambda e: e.tensor_scalar_mul(ihi[:, :], ieye32[:, :], shi32[:, 0:1]),
               [c6])
        c8 = v(lambda e: e.tensor_scalar_mul(ilo[:, :], ieye32[:, :], slo[:, 0:1]),
               [c7])
        self.tok_ident = [c1, c7, c8, c0]
        self.tok_psum7_free = c3

        self.NSW = 8
        self.NSB = 4
        self.wstage_free = [None] * self.NSW
        self.wbuf_free = [None] * self.NSB
        self.ring_idx = 0
        self.cast_flip = 0
        self.dmaq_flip = 0

        stream = td["x1"]; wstage = td["wstage"]
        self.x0tok = []
        for dc in range(NDC):
            toks = []
            for hh in range(2):
                sl = self.ring_idx % self.NSW
                self.ring_idx += 1
                stg = wstage[:, sl * 512:(sl + 1) * 512]
                dtok = em.dma(self._wq(), stg,
                              td["ut"][dc * 128:(dc + 1) * 128,
                                       hh * 512:(hh + 1) * 512],
                              deps=[self.wstage_free[sl]], inc=f"wl{sl}",
                              tag=f"x0 {dc} {hh}")
                dst = stream[:, dc * R_CORE + hh * 512: dc * R_CORE + (hh + 1) * 512]
                ctok = self._cast(dst, stg, [dtok], f"x0c {dc} {hh}")
                self.wstage_free[sl] = ctok
                toks.append(ctok)
            self.x0tok.append(toks)

    def build_ph1(self):
        nc, em, td = self.nc, self.em, self.t
        v = lambda fn, deps, inc="pv": em.on("vector", fn, deps=deps, inc=inc)
        s_ = lambda fn, deps, inc="psc": em.on("scalar", fn, deps=deps, inc=inc)
        pe = lambda fn, deps, inc="ppe": em.on("tensor", fn, deps=deps, inc=inc)

        stream = td["x1"]; hpool = td["hpool"]; sh = td["sh"]; sg = td["sg"]
        scratch = td["scratch"]; small = td["small"]
        ihi, ilo = td["ihi"], td["ilo"]; ones_bf = td["ones_bf"]
        onesr = td["onesr"]

        evac_tok = {}
        x1pre = lambda tc: hpool[:, tc * R_CORE:(tc + 1) * R_CORE]
        prev_stat = None
        pending_stats = []
        for tc in range(NDC):
            par = tc % 2
            psA = self.psum[2 * par]
            psB = self.psum[2 * par + 1]
            deps0 = [self.ph0_done] + self.tok_ident
            for sc in range(tc + 1):
                deps0 += self.x0tok[sc]
            if tc - 2 in evac_tok:
                deps0 += list(evac_tok[tc - 2])
            for sc in range(tc + 1):
                dlt = tc - sc
                lhs = sh[:, dlt * 128:dlt * 128 + 128]
                rhs0 = stream[:, sc * R_CORE: sc * R_CORE + 512]
                rhs1 = stream[:, sc * R_CORE + 512: (sc + 1) * R_CORE]
                st = (sc == 0)
                pe(lambda e, lhs=lhs, r=rhs0, p=psA, st=st:
                   e.matmul(p, lhs, r, start=st, stop=False),
                   deps0 if sc == 0 else [])
                pe(lambda e, lhs=lhs, r=rhs1, p=psB, st=st:
                   e.matmul(p, lhs, r, start=st, stop=False), [])
            rhs0 = stream[:, tc * R_CORE: tc * R_CORE + 512]
            rhs1 = stream[:, tc * R_CORE + 512: (tc + 1) * R_CORE]
            pe(lambda e, r=rhs0, p=psA: e.matmul(p, ihi[:, :], r,
                                                 start=False, stop=False), [])
            pe(lambda e, r=rhs1, p=psB: e.matmul(p, ihi[:, :], r,
                                                 start=False, stop=False), [])
            pe(lambda e, r=rhs0, p=psA: e.matmul(p, ilo[:, :], r,
                                                 start=False, stop=True), [])
            mlast = pe(lambda e, r=rhs1, p=psB: e.matmul(p, ilo[:, :], r,
                                                         start=False, stop=True), [])
            while pending_stats:
                prev_stat = pending_stats.pop(0)()
            xt = x1pre(tc)
            e0 = s_(lambda e, p=psA, xt=xt: e.copy(xt[:, 0:512], p), [mlast])
            e1 = s_(lambda e, p=psB, xt=xt: e.copy(xt[:, 512:1024], p), [])
            sqdep = [e0, e1] + ([prev_stat] if prev_stat else [])
            q0 = s_(lambda e, xt=xt: e.activation(sg[:, 0:512], xt[:, 0:512],
                                                  AF.Square), sqdep)
            q1 = s_(lambda e, xt=xt: e.activation(sg[:, 512:1024], xt[:, 512:1024],
                                                  AF.Square), [])
            evac_tok[tc] = (e0, e1)
            sdep = [e0, e1, q0, q1]
            if tc == 0:
                sdep += [self.ph0_done, self.tok_psum7_free]
            st = (tc == 0); sp = (tc == NDC - 1)

            def emit_stats(xt=xt, st=st, sp=sp, sdep=tuple(sdep)):
                pe(lambda e: e.matmul(self.psum[4][0:1, :], ones_bf[:, :],
                                      xt[:, 0:512], start=st, stop=sp), list(sdep))
                pe(lambda e: e.matmul(self.psum[5][0:1, :], ones_bf[:, :],
                                      xt[:, 512:1024], start=st, stop=sp), [])
                pe(lambda e: e.matmul(self.psum[6][0:1, :], ones_bf[:, :],
                                      sg[:, 0:512], start=st, stop=sp), [])
                return pe(lambda e: e.matmul(self.psum[7][0:1, :], ones_bf[:, :],
                                             sg[:, 512:1024], start=st, stop=sp), [])
            pending_stats.append(emit_stats)

        while pending_stats:
            prev_stat = pending_stats.pop(0)()

        RW = lambda i: scratch[0:1, i * 512:(i + 1) * 512]
        inv = 1.0 / D
        r0 = v(lambda e: e.tensor_scalar_mul(RW(0), self.psum[4][0:1, :], inv),
               [prev_stat, self.ph0_done])
        r1 = v(lambda e: e.tensor_scalar_mul(RW(1), self.psum[5][0:1, :], inv), [r0])
        r2 = v(lambda e: e.tensor_scalar_mul(RW(2), self.psum[6][0:1, :], inv), [r1])
        r3 = v(lambda e: e.tensor_scalar_mul(RW(3), self.psum[7][0:1, :], inv), [r2])
        self.ln1_psum_free = [r0, r1, r2, r3]
        toks_fin = []
        for c in range(2):
            mean, ex2, rstd, nmr = RW(0 + c), RW(2 + c), RW(4 + c), RW(6 + c)
            a0 = v(lambda e, mean=mean, rstd=rstd: e.tensor_mul(rstd, mean, mean),
                   [r3])
            a1 = v(lambda e, ex2=ex2, rstd=rstd, nmr=nmr: e.tensor_sub(nmr, ex2, rstd),
                   [a0])
            a2 = v(lambda e, nmr=nmr, rstd=rstd: e.tensor_scalar_add(rstd, nmr, EPS),
                   [a1])
            a3 = s_(lambda e, rstd=rstd, nmr=nmr: e.activation(nmr, rstd, AF.Ln),
                    [a2])
            a4 = s_(lambda e, nmr=nmr, rstd=rstd: e.activation(rstd, nmr, AF.Exp,
                                                               scale=-0.5), [a3])
            a5 = v(lambda e, mean=mean, rstd=rstd, nmr=nmr: e.tensor_mul(
                nmr, mean, rstd), [a4])
            toks_fin += [a4, a5]
        bc = []
        bdeps = toks_fin + list(evac_tok[NDC - 1]) + list(evac_tok[NDC - 2])
        for i, src_i in enumerate([4, 5, 6, 7]):
            ps = self.psum[i][:, :]
            bc.append(pe(lambda e, ps=ps, src_i=src_i: e.matmul(
                ps, onesr[0:1, :], RW(src_i), start=True, stop=True), bdeps))
        lnb = self.sb("lnb", [128, 2048], BF16)
        bcv = []
        for i in range(4):
            dst = lnb[:, i * 512:(i + 1) * 512]
            ps = self.psum[i][:, :]
            if i % 2 == 0:
                bcv.append(v(lambda e, dst=dst, ps=ps: e.tensor_copy(dst, ps),
                             [bc[i], r3]))
            else:
                bcv.append(s_(lambda e, dst=dst, ps=ps: e.copy(dst, ps),
                              [bc[i], r3]))
        self.ln1_bcast_psum_free = bcv
        rstd_b = lnb[:, 0:1024]
        nmr_b = lnb[:, 1024:2048]
        self.ln1_norm_tok = []
        prev_n1 = None
        tmp_bf = sg
        for tc in range(NDC):
            xt = x1pre(tc)
            dst = stream[:, tc * R_CORE:(tc + 1) * R_CORE]
            n0 = v(lambda e, xt=xt: e.tensor_mul(tmp_bf[:, :], xt, rstd_b),
                   bcv + ([prev_n1] if prev_n1 else []))
            n1 = v(lambda e, dst=dst: e.tensor_sub(dst, tmp_bf[:, :], nmr_b), [n0])
            prev_n1 = n1
            self.ln1_norm_tok.append(n1)
        if self.debug_int:
            x1o = nc.dram_tensor("x1_out", [128, NDC * R_CORE], BF16,
                                 kind="ExternalOutput").ap()
            self.out_toks = getattr(self, "out_toks", []) + [em.dma(
                "sync", x1o[:, :], stream[:, :], deps=[self.ln1_norm_tok[-1]],
                inc="dbg", tag="x1dump")]

    def build_ph2(self, H):
        nc, em, td = self.nc, self.em, self.t
        v = lambda fn, deps, inc="pv": em.on("vector", fn, deps=deps, inc=inc)
        s_ = lambda fn, deps, inc="psc": em.on("scalar", fn, deps=deps, inc=inc)
        pe = lambda fn, deps, inc="ppe": em.on("tensor", fn, deps=deps, inc=inc)
        sb = self.sb

        stream = td["x1"]; hpool = td["hpool"]; sg = td["sg"]
        wstage = td["wstage"]; wbuf = td["wbuf"]
        scratch = td["scratch"]; scratch2 = td["scratch2"]
        i128bf = td["i128bf"]; wgT = td["wgT"]; wdT = td["wdT"]
        small = td["small"]
        lnstat = sb("lnstat", [128, 24])
        stmp = sb("stmp", [128, 8])
        hcol = lambda ic: hpool[:, ic * R_HALF:(ic + 1) * R_HALF]
        xsl = lambda dc: stream[:, dc * R_CORE + H * 512:
                                dc * R_CORE + H * 512 + 512]

        if H == 0:
            h_guard = None
            psum_guard = {0: self.ln1_bcast_psum_free, 1: self.ln1_psum_free}
        else:
            h_guard = [self.h_read_done]
            psum_guard = {0: self.evac2b_last[0], 1: self.evac2b_last[1]}

        def xg(dc):
            return [self.ln1_norm_tok[dc]] if H == 0 else []

        def hg(ic):
            if H == 0:
                return [self.ln1_norm_tok[ic // 2]] if ic < 2 * NDC else []
            return h_guard

        evac_all = {}
        evac_sg_free = {}
        NOG = NIC // 2
        for og in range(NOG):
            par = og % 2
            pg0, pg1 = self.psum[4 * par], self.psum[4 * par + 1]
            pu0, pu1 = self.psum[4 * par + 2], self.psum[4 * par + 3]
            if og < 2:
                first_deps = list(psum_guard[og])
            else:
                first_deps = list(evac_all[og - 2])
            last_mm = None
            wg2 = wgT.rearrange("d (r o) -> d r o", r=2)
            for dc in range(NDC):
                sl = self.ring_idx % self.NSW
                bl = self.ring_idx % self.NSB
                self.ring_idx += 1
                stg = wstage[:, sl * 512:(sl + 1) * 512]
                wb = wbuf[:, bl * 512:(bl + 1) * 512]
                d0 = em.dma(self._wq(), stg.rearrange("p (r o) -> p r o", r=2),
                            wg2[dc * 128:(dc + 1) * 128, :,
                                og * 256:(og + 1) * 256],
                            deps=[self.wstage_free[sl]], inc=f"wl{sl}",
                            tag=f"2a {H} {og} {dc}")
                ctok = self._cast(wb, stg, [d0, self.wbuf_free[bl]],
                                  f"2a c {H} {og} {dc}",
                                  force=("scalar" if H == 0 else "vector")
                                  if og < 2 else None)
                self.wstage_free[sl] = ctok
                st = (dc == 0); sp = (dc == NDC - 1)
                mdeps = [ctok] + (first_deps if dc == 0 else []) + \
                    (xg(dc) if og == 0 else []) + \
                    (h_guard if (H == 1 and og == 0 and dc == 0) else [])
                pe(lambda e, wb=wb, dc=dc, p=pg0, st=st, sp=sp: e.matmul(
                    p, wb[:, 0:128], xsl(dc), start=st, stop=sp), mdeps)
                pe(lambda e, wb=wb, dc=dc, p=pg1, st=st, sp=sp: e.matmul(
                    p, wb[:, 128:256], xsl(dc), start=st, stop=sp), [])
                pe(lambda e, wb=wb, dc=dc, p=pu0, st=st, sp=sp: e.matmul(
                    p, wb[:, 256:384], xsl(dc), start=st, stop=sp), [])
                last_mm = pe(lambda e, wb=wb, dc=dc, p=pu1, st=st, sp=sp: e.matmul(
                    p, wb[:, 384:512], xsl(dc), start=st, stop=sp), [])
                self.wbuf_free[bl] = last_mm
            ic0, ic1 = 2 * og, 2 * og + 1
            sdep = [last_mm] + (evac_sg_free[og - 1] if og >= 1 else
                                ([self.ln1_norm_tok[-1]] if H == 0 else []))
            s0 = s_(lambda e, p=pg0: e.activation(sg[:, 0:512], p, AF.Silu), sdep)
            s1 = s_(lambda e, p=pg1: e.activation(sg[:, 512:1024], p, AF.Silu), [])
            u0 = v(lambda e, p=pu0, ic0=ic0: e.tensor_tensor(
                hcol(ic0), sg[:, 0:512], p, ALU.mult),
                [s0, last_mm] + (hg(ic0) if og < 2 or H == 0 else []))
            u1 = v(lambda e, p=pu1, ic1=ic1: e.tensor_tensor(
                hcol(ic1), sg[:, 512:1024], p, ALU.mult),
                [s1] + (hg(ic1) if og < 2 or H == 0 else []))
            evac_all[og] = (s0, s1, u0, u1)
            evac_sg_free[og] = [u0, u1]
            if og == 1 and getattr(self, "pending_norm", None):
                self.build_ln2_norm()

        z0 = v(lambda e: e.memset(lnstat[:, 0:8], 0.0),
               [getattr(self, "ln2_stats_done", None)])
        t_dram = td["t_dram"]
        evac2b = {}
        evac2b_free = {}
        tguard = getattr(self, "tnorm_load_toks", None)
        tstores = {rt: [] for rt in range(4)}
        h_last_mm = None
        for hp in range(8):
            par = hp % 2
            zt = [self.psum[4 * par + rt] for rt in range(4)]
            if hp == 0:
                first_deps = list(evac_all[NOG - 1])
            elif hp == 1:
                first_deps = list(evac_all[NOG - 2])
            else:
                first_deps = list(evac2b_free[hp - 2])
            for rt in range(4):
                for j in range(4):
                    dsub = 4 * hp + j
                    lhs = stream[:, dsub * R_CORE + H * 512 + rt * 128:
                                 dsub * R_CORE + H * 512 + (rt + 1) * 128]
                    st = (j == 0)
                    dd = first_deps if (rt == 0 and j == 0) else []
                    pe(lambda e, z=zt[rt], rt=rt, j=j, lhs=lhs, st=st: e.matmul(
                        z[:, j * 128:(j + 1) * 128], lhs, i128bf[:, :],
                        start=st, stop=False, skip_group_check=True), dd)
            for ic in range(NIC):
                sl = self.ring_idx % self.NSW
                bl = self.ring_idx % self.NSB
                self.ring_idx += 1
                stg = wstage[:, sl * 512:(sl + 1) * 512]
                wb = wbuf[:, bl * 512:(bl + 1) * 512]
                d0 = em.dma(self._wq(), stg,
                            wdT[ic * 128:(ic + 1) * 128, hp * 512:(hp + 1) * 512],
                            deps=[self.wstage_free[sl]], inc=f"wl{sl}",
                            tag=f"2b {H} {hp} {ic}")
                ctok = self._cast(wb, stg, [d0, self.wbuf_free[bl]],
                                  f"2b c {H} {hp} {ic}")
                self.wstage_free[sl] = ctok
                sp = (ic == NIC - 1)
                for rt in range(4):
                    mm = pe(lambda e, wb=wb, ic=ic, rt=rt, z=zt[rt], sp=sp: e.matmul(
                        z, hcol(ic)[:, rt * 128:(rt + 1) * 128], wb,
                        start=False, stop=sp, skip_group_check=True),
                        [ctok] if rt == 0 else [])
                    if rt == 3:
                        self.wbuf_free[bl] = mm
                        h_last_mm = mm
            etoks = []
            ftoks = []
            for rt in range(4):
                sl2 = rt
                slot = scratch2[:, sl2 * 512:(sl2 + 1) * 512]
                prevf = evac2b_free.get(hp - 1)
                prevd = tstores[rt][-1] if tstores[rt] else None
                nog = getattr(self, "norm_out_last", None)
                wdep = [h_last_mm] + ([prevf[rt]] if prevf else []) + \
                    ([prevd] if prevd else []) + \
                    (([tguard[4 * rt + 3]] if tguard else []) +
                     ([nog[rt // 2]] if nog else []) if hp == 0 else [])
                c0 = s_(lambda e, slot=slot, z=zt[rt], rt=rt: e.activation(
                    slot, z, AF.Copy, accum_out=stmp[:, rt:rt + 1]), wdep)
                q0 = s_(lambda e, z=zt[rt], rt=rt: e.activation(
                    small[:, 0:512] if rt % 2 == 0 else small[:, 512:1024], z,
                    AF.Square, accum_out=stmp[:, 4 + rt:5 + rt]), [c0])
                a0 = v(lambda e, rt=rt: e.tensor_add(
                    lnstat[:, rt:rt + 1], lnstat[:, rt:rt + 1],
                    stmp[:, rt:rt + 1]), [c0, z0])
                a1 = v(lambda e, rt=rt: e.tensor_add(
                    lnstat[:, 4 + rt:5 + rt], lnstat[:, 4 + rt:5 + rt],
                    stmp[:, 4 + rt:5 + rt]), [q0, a0])
                dout = em.dma("scalar", t_dram[rt * 128:(rt + 1) * 128,
                                               hp * 512:(hp + 1) * 512], slot,
                              deps=[c0], inc=f"ts{rt}", tag=f"t {H} {hp} {rt}")
                tstores[rt].append(dout)
                etoks += [a1, dout]
                ftoks.append(a1)
            evac2b[hp] = etoks
            evac2b_free[hp] = ftoks
        self.evac2b_last = {0: evac2b_free[6], 1: evac2b_free[7]}
        self.h_read_done = h_last_mm
        if self.debug_int and H == 0:
            t0o = nc.dram_tensor("t0_out", [R_HALF, D], F32,
                                 kind="ExternalOutput").ap()
            tdeps = [tstores[rt][-1] for rt in range(4)]
            self.out_toks = getattr(self, "out_toks", []) + [em.dma(
                "sync", t0o[:, :], t_dram[:, :], deps=tdeps, inc="dbg", tag="t0dump")]

        self.pending_norm = dict(H=H, evac2b_free=evac2b_free, tstores=tstores,
                                 lnstat=lnstat, zt_dummy=None)
        return

    def build_ln2_norm(self):
        ctx = self.pending_norm
        self.pending_norm = None
        H = ctx["H"]; evac2b_free = ctx["evac2b_free"]; tstores = ctx["tstores"]
        lnstat = ctx["lnstat"]
        nc, em, td = self.nc, self.em, self.t
        v = lambda fn, deps, inc="pv": em.on("vector", fn, deps=deps, inc=inc)
        s_ = lambda fn, deps, inc="psc": em.on("scalar", fn, deps=deps, inc=inc)
        scratch = td["scratch"]; scratch2 = td["scratch2"]
        t_dram = td["t_dram"]
        inv = 1.0 / D
        out = td["out"]
        fin_dep = evac2b_free[7] + ([self.last_norm_act] if
                                    getattr(self, "last_norm_act", None) else [])
        self.tnorm_load_toks = []
        self.tpiece_free = [None] * 4
        self.out_toks = getattr(self, "out_toks", [])
        last_b8 = None
        prev_ltok = None
        for rt in range(4):
            mean = lnstat[:, 8 + rt:9 + rt]
            rstd = lnstat[:, 12 + rt:13 + rt]
            nmr = lnstat[:, 16 + rt:17 + rt]
            b0 = v(lambda e, rt=rt, mean=mean: e.tensor_scalar_mul(
                mean, lnstat[:, rt:rt + 1], inv), fin_dep)
            b1 = v(lambda e, rt=rt, rstd=rstd: e.tensor_scalar_mul(
                rstd, lnstat[:, 4 + rt:5 + rt], inv), [b0])
            b2 = v(lambda e, mean=mean, nmr=nmr: e.tensor_mul(nmr, mean, mean), [b1])
            b3 = v(lambda e, rstd=rstd, nmr=nmr: e.tensor_sub(rstd, rstd, nmr), [b2])
            b4 = v(lambda e, rstd=rstd: e.tensor_scalar_add(rstd, rstd, EPS), [b3])
            b5 = s_(lambda e, rstd=rstd, nmr=nmr: e.activation(nmr, rstd, AF.Ln),
                    [b4])
            b6 = s_(lambda e, rstd=rstd, nmr=nmr: e.activation(rstd, nmr, AF.Exp,
                                                               scale=-0.5), [b5])
            b7 = v(lambda e, mean=mean, rstd=rstd, nmr=nmr: e.tensor_mul(
                nmr, mean, rstd), [b6])
            b8 = v(lambda e, nmr=nmr: e.tensor_scalar_mul(nmr, nmr, -1.0), [b7])
            last_b8 = b8
            base_dep = [b8] + tstores[rt] + \
                ([self.ln1_norm_tok[-1]] if H == 0 and rt == 0 else []) + \
                ([self.last_norm_act] if (H == 1 and rt == 0 and
                                          getattr(self, "last_norm_act", None))
                 else [])
            piece_toks = []
            for q in range(4):
                pd = list(base_dep) + \
                    ([self.tpiece_free[q]] if self.tpiece_free[q] else [])
                ptok = em.dma("scalar", scratch[:, q * 1024:(q + 1) * 1024],
                              t_dram[rt * 128:(rt + 1) * 128,
                                     q * 1024:(q + 1) * 1024],
                              deps=pd, inc=f"tp{q}", tag=f"tl {H} {rt} {q}")
                piece_toks.append(ptok)
            self.tnorm_load_toks += piece_toks
            prev_out = [None] * 2
            last_n0 = None
            for cc in range(4):
                sl2 = cc % 2
                slot = scratch2[:, sl2 * 1024:(sl2 + 1) * 1024]
                ndep = [piece_toks[cc], b8] + \
                    ([prev_out[sl2]] if prev_out[sl2] else
                     ([tstores[2 * sl2][-1], tstores[2 * sl2 + 1][-1]]
                      if rt == 0 else []))
                n0 = v(lambda e, slot=slot, cc=cc, rstd=rstd, nmr=nmr:
                       e.tensor_scalar(slot,
                                       scratch[:, cc * 1024:(cc + 1) * 1024],
                                       rstd, nmr, ALU.mult, ALU.add), ndep)
                o0 = em.dma("scalar",
                            out[H * 512 + rt * 128: H * 512 + (rt + 1) * 128,
                                cc * 1024:(cc + 1) * 1024], slot,
                            deps=[n0], inc=f"os{sl2}", tag=f"o {H} {rt} {cc}")
                prev_out[sl2] = o0
                self.tpiece_free[cc] = n0
                last_n0 = n0
                self.norm_out_last = getattr(self, "norm_out_last", {})
                self.norm_out_last[sl2] = o0
                self.out_toks.append(o0)
            prev_ltok = last_n0
            self.last_norm_act = last_n0
            self.ln2_stats_done = last_b8

    def finish(self, extra_deps=()):
        em = self.em
        deps = list(extra_deps) + list(getattr(self, "out_toks", []))
        if self.debug_ph0:
            deps += self.dbg_toks
        def fin(e):
            return None
        em.on("sync", fin, deps=deps, tag="final")
        with self.nc.Block() as block:
            em.finalize(block)
        return self.nc


def host_inputs_common(inp):
    d = {
        "wgT": np.ascontiguousarray(inp["L0_ffn_w_gate_up"].T),
        "wdT": np.ascontiguousarray(inp["L0_ffn_w_down"].T),
        "lam": np.asarray(inp["L0_attn_Lambda_ri"]),
        "pri": np.asarray(inp["L0_attn_p_ri"]),
        "qri": np.asarray(inp["L0_attn_q_ri"]),
        "vc": np.asarray(inp["L0_attn_Vc_ri"]),
        "ct": np.asarray(inp["L0_attn_Ct_ri"]),
        "bp": np.asarray(inp["L0_attn_Bp"]),
        "dskip": np.asarray(inp["L0_attn_D"]),
        "lstep": np.asarray(inp["L0_attn_log_step"]),
    }
    d.update(make_constants())
    return d


_NC_CACHE = {}


def build_full():
    if "full" not in _NC_CACHE:
        b = Builder()
        b.build_ph0()
        b.build_prep()
        b.build_ph1()
        b.build_ph2(0)
        b.build_ph2(1)
        b.build_ln2_norm()
        _NC_CACHE["full"] = b.finish()
    return _NC_CACHE["full"]


def kernel(**inputs):
    nc = build_full()
    common = host_inputs_common(inputs)
    u = np.asarray(inputs["L0_attn_u"])
    uT = np.ascontiguousarray(u.reshape(8192, 4096).T)
    in_maps = []
    for c in range(N_CORES):
        m = dict(common)
        m["ut"] = np.ascontiguousarray(uT[:, c * R_CORE:(c + 1) * R_CORE])
        in_maps.append(m)
    res = run_bass_kernel_spmd(nc, in_maps, core_ids=list(range(N_CORES)))
    outs = [res.results[c]["out"] for c in range(N_CORES)]
    full = np.concatenate(outs, axis=0).reshape(1, 8192, 4096).astype(np.float32)
    return full
